# version 02
# baseline (speedup 1.0000x reference)
import numpy as np
import ml_dtypes
from contextlib import ExitStack
import concourse.bass as bass
import concourse.mybir as mybir
from concourse.bass_utils import run_bass_kernel_spmd

F32 = mybir.dt.float32
BF16 = mybir.dt.bfloat16
I32 = mybir.dt.int32
AF = mybir.ActivationFunctionType
ALU = mybir.AluOpType
AX = mybir.AxisListType

B, T, C, H, HD = 4, 2048, 4096, 32, 128
NHEADS = 16
CH = C // 2
NCB = C // 128
NTS = T // 512
QKVO = 3 * CH
NOG = QKVO // 128
S_SCALE = float(1.0 / np.sqrt(HD))
A_SCALE = float(1.0 / np.sqrt(C))
EPS = 1e-5


def _host_consts():
    th = 10000.0 ** (-np.arange(0, HD, 2, dtype=np.float64) / HD)
    t = np.arange(T, dtype=np.float64)
    ang = np.outer(th, t)
    cosfm = np.concatenate([np.cos(ang), np.cos(ang)], 0).astype(np.float32)
    sinfm = np.concatenate([-np.sin(ang), np.sin(ang)], 0).astype(np.float32)
    masks = np.zeros((128, 4 * 512), np.float32)
    ii = np.arange(128)[:, None]
    jj = np.arange(512)[None, :]
    for r in range(4):
        masks[:, r * 512:(r + 1) * 512] = (128 * r + ii <= jj)
    ident = np.eye(128, dtype=np.float32)
    ones1 = np.ones((1, 128), np.float32)
    a48 = (np.arange(48) // 16).astype(np.float32).reshape(48, 1)
    return (cosfm, sinfm, masks.astype(ml_dtypes.bfloat16), ident, ones1, a48)


def build_nc(debug=False, upto=9):
    nc = bass.Bass("TRN2", target_bir_lowering=False)
    scratch_kind = "ExternalOutput" if debug else "Internal"

    xT = nc.dram_tensor("xT", [C, T], F32, kind="ExternalInput")
    wqkvT = nc.dram_tensor("wqkvT", [C, QKVO], F32, kind="ExternalInput")
    wprojT = nc.dram_tensor("wprojT", [C, CH], F32, kind="ExternalInput")
    wffnT = nc.dram_tensor("wffnT", [C, CH], F32, kind="ExternalInput")
    biasT = nc.dram_tensor("biasT", [128, 16], F32, kind="ExternalInput")
    loraAT = nc.dram_tensor("loraAT", [C, 48], F32, kind="ExternalInput")
    loraBT = nc.dram_tensor("loraBT", [48, CH], F32, kind="ExternalInput")
    adT = nc.dram_tensor("adT", [C, 10], F32, kind="ExternalInput")
    gat = nc.dram_tensor("gat", [1, 1], F32, kind="ExternalInput")
    lsi = nc.dram_tensor("lsi", [1, 4], I32, kind="ExternalInput")
    bvec = nc.dram_tensor("bvec", [1, 1], F32, kind="ExternalInput")
    cosd = nc.dram_tensor("cosd", [128, T], F32, kind="ExternalInput")
    sind = nc.dram_tensor("sind", [128, T], F32, kind="ExternalInput")
    maskd = nc.dram_tensor("maskd", [128, 2048], BF16, kind="ExternalInput")
    identd = nc.dram_tensor("identd", [128, 128], F32, kind="ExternalInput")
    ones1d = nc.dram_tensor("ones1d", [1, 128], F32, kind="ExternalInput")
    a48d = nc.dram_tensor("a48d", [48, 1], F32, kind="ExternalInput")
    onesbd = nc.dram_tensor("onesbd", [128, 1], BF16, kind="ExternalInput")
    zerod = nc.dram_tensor("zerod", [1, 64], F32, kind="ExternalInput")

    outT = nc.dram_tensor("outT", [CH, T], F32, kind="ExternalOutput")

    xnscr = nc.dram_tensor("xnscr", [C, T], BF16, kind=scratch_kind)
    qscr = nc.dram_tensor("qscr", [CH, T], BF16, kind=scratch_kind)
    kscr = nc.dram_tensor("kscr", [CH, T], BF16, kind=scratch_kind)
    vscr = nc.dram_tensor("vscr", [CH, T], BF16, kind=scratch_kind)
    yscr = nc.dram_tensor("yscr", [T, CH], BF16)
    agy = nc.dram_tensor("agy", [2 * T, CH], BF16)
    fspill = nc.dram_tensor("fspill", [CH, T], F32, kind=scratch_kind)
    agy_dbg = nc.dram_tensor("agy_dbg", [2 * T, CH], BF16, kind="ExternalOutput") if debug else None

    stack = ExitStack()
    nc._kernel_stack = stack

    def sb(name, shape, dt):
        return stack.enter_context(nc.sbuf_tensor(name, shape, dt))

    bank = [stack.enter_context(nc.psum_tensor(f"bank{i}", [128, 512], F32))
            for i in range(8)]
    Dk, Ek, Fk, Gk, Hk = bank[3], bank[4], bank[5], bank[6], bank[7]

    ones_bf = sb("ones_bf", [128, 1], BF16)
    ident_sb = sb("ident_sb", [128, 128], F32)
    ones1_sb = sb("ones1_sb", [1, 128], F32)
    a48_sb = sb("a48_sb", [48, 1], F32)
    mask_sb = sb("mask_sb", [128, 2048], BF16)
    lsi_sb = sb("lsi_sb", [1, 4], F32)
    bv_sb = sb("bv_sb", [1, 1], F32)
    gat_sb = sb("gat_sb", [1, 1], F32)
    biasT_sb = sb("biasT_sb", [128, 16], F32)
    loraA_sb = sb("loraA_sb", [128, NCB * 48], BF16)
    loraB_sb = sb("loraB_sb", [48, CH], BF16)
    adT_sb = sb("adT_sb", [128, NCB * 10], BF16)
    m48_sb = sb("m48_sb", [48, 1], F32)
    scr1 = sb("scr1", [1, 64], F32)
    scr48 = sb("scr48", [48, 4], F32)
    gat_bc = sb("gat_bc", [128, 1], F32)
    ak_sb = sb("ak_sb", [128, NHEADS * 10], BF16)
    av_sb = sb("av_sb", [10, NHEADS * 128], BF16)
    avtmp = sb("avtmp", [128, 10], F32)
    aksq_sb = sb("aksq_sb", [128, 160], BF16)
    nrm_sb = sb("nrm_sb", [1, 64], F32)
    nbh_bc = sb("nbh_bc", [128, 16], F32)
    nbha_bc = sb("nbha_bc", [10, 16], F32)

    sems = {}

    class S:
        def __init__(self, name):
            self.h = stack.enter_context(nc.semaphore(name))
            self.n = 0

        def inc(self, instr, k=1):
            instr.then_inc(self.h, k)
            self.n += k
            return self.n
    sPE, sACT, sDVE, sCC = S("sPE"), S("sACT"), S("sDVE"), S("sCC")
    d0 = S("d0")
    dC2 = S("dC2")
    dX = [S(f"dX{i}") for i in range(4)]
    dSP = S("dSP")
    dWB = [S(f"dWB{i}") for i in range(2)]
    dST = [S(f"dST{i}") for i in range(4)]
    dKV = S("dKV")
    dQ = [S(f"dQ{i}") for i in range(2)]
    dY = [S(f"dY{i}") for i in range(2)]
    dXN2 = S("dXN2")
    dW4 = [S(f"dW4{i}") for i in range(2)]
    dXR = [S(f"dXR{i}") for i in range(2)]
    dFS = [S(f"dFS{i}") for i in range(2)]
    dYF = S("dYF")
    dW5 = [S(f"dW5{i}") for i in range(2)]
    dPL = [S(f"dPL{i}") for i in range(2)]
    dOUT = [S(f"dOUT{i}") for i in range(2)]
    dDBG = S("dDBG")
    ALL_SEMS = ([d0, dC2, dSP, dKV, dXN2, dYF, dDBG, sPE, sACT, sDVE]
                + dX + dWB + dST + dQ + dY + dW4 + dXR + dFS + dW5 + dPL + dOUT)

    blk = stack.enter_context(nc.Block())
    PE, ACT, DVE, SP, POOL = blk.tensor, blk.scalar, blk.vector, blk.sync, blk.gpsimd

    M = {}

    def p0_pool(g):
        for dst, src in [(ident_sb, identd), (ones1_sb, ones1d), (a48_sb, a48d),
                         (mask_sb, maskd), (ones_bf, onesbd), (nrm_sb, zerod),
                         (bv_sb, bvec), (gat_sb, gat), (biasT_sb, biasT)]:
            d0.inc(g.dma_start(dst[:], src[:, :]), 16)
        d0.inc(g.dma_start(lsi_sb[:], lsi[:, :]), 16)
        d0.inc(g.dma_start(bass.AP(loraA_sb, 0, [[NCB * 48, 128], [48, NCB], [1, 48]]),
                           bass.AP(loraAT, 0, [[48, 128], [128 * 48, NCB], [1, 48]])), 16)
        d0.inc(g.dma_start(loraB_sb[:], loraBT[:, :]), 16)
        d0.inc(g.dma_start(bass.AP(adT_sb, 0, [[NCB * 10, 128], [10, NCB], [1, 10]]),
                           bass.AP(adT, 0, [[10, 128], [128 * 10, NCB], [1, 10]])), 16)
        M["p0_dma"] = d0.n
    POOL(p0_pool)

    def p0_dve1(v):
        v.wait_ge(d0.h, M["p0_dma"])

        def c_(instr):
            sDVE.inc(instr)
            v.wait_ge(sDVE.h, sDVE.n)
        c_(v.tensor_single_scalar(scr1[0:1, 0:4], lsi_sb[0:1, 0:4], bv_sb[0:1, 0:1], ALU.is_le))
        c_(v.tensor_scalar_add(scr1[0:1, 4:5], scr1[0:1, 3:4], 3.0))
        c_(v.tensor_mul(scr1[0:1, 5:6], scr1[0:1, 2:3], scr1[0:1, 4:5]))
        c_(v.tensor_scalar_mul(scr1[0:1, 6:7], scr1[0:1, 1:2], 2.0))
        c_(v.tensor_scalar(scr1[0:1, 7:8], scr1[0:1, 1:2], -1.0, 1.0, ALU.mult, ALU.add))
        c_(v.tensor_mul(scr1[0:1, 8:9], scr1[0:1, 7:8], scr1[0:1, 0:1]))
        c_(v.tensor_add(scr1[0:1, 9:10], scr1[0:1, 6:7], scr1[0:1, 8:9]))
        c_(v.tensor_scalar(scr1[0:1, 10:11], scr1[0:1, 2:3], -1.0, 1.0, ALU.mult, ALU.add))
        c_(v.tensor_mul(scr1[0:1, 11:12], scr1[0:1, 10:11], scr1[0:1, 9:10]))
        c_(v.tensor_add(scr1[0:1, 12:13], scr1[0:1, 5:6], scr1[0:1, 11:12]))
        c_(v.tensor_scalar_add(scr1[0:1, 13:14], scr1[0:1, 12:13], -1.0))
        c_(v.tensor_scalar(scr1[0:1, 14:15], scr1[0:1, 13:14], -1.0, 0.0, ALU.add, ALU.max))
        c_(v.tensor_scalar_min(scr1[0:1, 15:16], scr1[0:1, 14:15], 2.0))
        c_(v.tensor_single_scalar(scr1[0:1, 16:17], scr1[0:1, 13:14], 0.5, ALU.is_ge))
        M["p0_dve1"] = sDVE.n
    DVE(p0_dve1)

    def p0_pe1(t):
        t.wait_ge(sDVE.h, M["p0_dve1"])
        t.matmul(Ek[0:48, 0:1], ones1_sb[0:1, 0:48], scr1[0:1, 15:16], start=True, stop=True)
        t.matmul(Ek[0:48, 1:2], ones1_sb[0:1, 0:48], scr1[0:1, 16:17], start=True, stop=True)
        sPE.inc(t.matmul(Fk[0:128, 0:1], ones1_sb[0:1, 0:128], gat_sb[0:1, 0:1],
                         start=True, stop=True))
        M["p0_pe1"] = sPE.n
    PE(p0_pe1)

    def p0_dve2(v):
        v.wait_ge(sPE.h, M["p0_pe1"])

        def c_(instr):
            sDVE.inc(instr)
            v.wait_ge(sDVE.h, sDVE.n)
        c_(v.tensor_copy(gat_bc[:], Fk[0:128, 0:1]))
        c_(v.tensor_sub(scr48[0:48, 0:1], a48_sb[0:48, 0:1], Ek[0:48, 0:1]))
        c_(v.tensor_scalar_mul(scr48[0:48, 1:2], scr48[0:48, 0:1], -1.0))
        c_(v.tensor_max(scr48[0:48, 1:2], scr48[0:48, 0:1], scr48[0:48, 1:2]))
        c_(v.tensor_scalar(scr48[0:48, 2:3], scr48[0:48, 1:2], -1.0, 1.0, ALU.mult, ALU.add))
        c_(v.tensor_scalar_max(scr48[0:48, 3:4], scr48[0:48, 2:3], 0.0))
        c_(v.tensor_mul(m48_sb[0:48, 0:1], scr48[0:48, 3:4], Ek[0:48, 1:2]))
        M["p0_mask"] = sDVE.n
    DVE(p0_dve2)

    g12 = ExitStack()
    xn = g12.enter_context(nc.sbuf_tensor("xn", [128, NCB * T], BF16))
    g1 = ExitStack()
    m_bc = g1.enter_context(nc.sbuf_tensor("m_bc", [128, T], F32))
    r_bc = g1.enter_context(nc.sbuf_tensor("r_bc", [128, T], F32))
    sqb = [g1.enter_context(nc.sbuf_tensor(f"sqb{i}", [128, T], BF16)) for i in range(2)]
    mrow = g1.enter_context(nc.sbuf_tensor("mrow", [1, T], F32))
    vrow = g1.enter_context(nc.sbuf_tensor("vrow", [1, T], F32))
    rrow = g1.enter_context(nc.sbuf_tensor("rrow", [1, T], F32))

    xld = []

    def p1_load(g):
        g.wait_ge(d0.h, M["p0_dma"])
        for cb in range(NCB):
            s_ = dX[cb % 4]
            if cb >= 4:
                g.wait_ge(s_.h, s_.n)
            s_.inc(g.dma_start(xn[:, cb * T:(cb + 1) * T], xT[cb * 128:(cb + 1) * 128, :]), 16)
            xld.append((s_.h, s_.n))
    POOL(p1_load)

    sq_done, sq_cons = {}, {}
    for cb in range(NCB):
        def p1_sq(s, cb=cb):
            s.wait_ge(*xld[cb])
            if cb >= 2:
                s.wait_ge(sPE.h, sq_cons[cb - 2])
            sACT.inc(s.activation(sqb[cb % 2][:, :], xn[:, cb * T:(cb + 1) * T], AF.Square))
            sq_done[cb] = sACT.n
        ACT(p1_sq)

        def p1_mm(t, cb=cb):
            st, sp = (cb == 0), (cb == NCB - 1)
            t.wait_ge(*xld[cb])
            if cb == 0:
                t.wait_ge(sDVE.h, M["p0_mask"])
            for tsl in range(NTS):
                t.matmul(bank[tsl][0:1, :], ones_bf[:, 0:1],
                         xn[:, cb * T + tsl * 512:cb * T + (tsl + 1) * 512],
                         start=st, stop=sp)
            t.wait_ge(sACT.h, sq_done[cb])
            ins = None
            for tsl in range(NTS):
                ins = t.matmul(bank[4 + tsl][0:1, :], ones_bf[:, 0:1],
                               sqb[cb % 2][:, tsl * 512:(tsl + 1) * 512],
                               start=st, stop=sp)
            sPE.inc(ins)
            sq_cons[cb] = sPE.n
        PE(p1_mm)
    M["p1_stats"] = sPE.n

    def p1_stats(v):
        v.wait_ge(sPE.h, M["p1_stats"])

        def c_(instr):
            sDVE.inc(instr)
            v.wait_ge(sDVE.h, sDVE.n)
        for tsl in range(NTS):
            sl = slice(tsl * 512, (tsl + 1) * 512)
            c_(v.tensor_scalar_mul(mrow[0:1, sl], bank[tsl][0:1, :], 1.0 / C))
            c_(v.tensor_mul(vrow[0:1, sl], mrow[0:1, sl], mrow[0:1, sl]))
            c_(v.scalar_tensor_tensor(vrow[0:1, sl], bank[4 + tsl][0:1, :], 1.0 / C,
                                      vrow[0:1, sl], ALU.mult, ALU.subtract))
        M["p1_var"] = sDVE.n
    DVE(p1_stats)

    def p1_rstd(s):
        s.wait_ge(sDVE.h, M["p1_var"])
        sACT.inc(s.activation(rrow[0:1, :], vrow[0:1, :], AF.Ln, bias=np.float32(EPS), scale=np.float32(1.0)))
        s.wait_ge(sACT.h, sACT.n)
        sACT.inc(s.activation(rrow[0:1, :], rrow[0:1, :], AF.Exp, scale=np.float32(-0.5)))
        M["p1_rstd"] = sACT.n
    ACT(p1_rstd)

    def p1_bcast(t):
        t.wait_ge(sACT.h, M["p1_rstd"])
        ins = None
        for tsl in range(NTS):
            sl = slice(tsl * 512, (tsl + 1) * 512)
            t.matmul(bank[tsl][:, :], ones1_sb[0:1, 0:128], mrow[0:1, sl], start=True, stop=True)
            ins = t.matmul(bank[4 + tsl][:, :], ones1_sb[0:1, 0:128], rrow[0:1, sl],
                           start=True, stop=True)
        sPE.inc(ins)
        M["p1_bc"] = sPE.n
    PE(p1_bcast)

    def p1_bcsb(v):
        v.wait_ge(sPE.h, M["p1_bc"])
        for tsl in range(NTS):
            sl = slice(tsl * 512, (tsl + 1) * 512)
            sDVE.inc(v.tensor_copy(m_bc[:, sl], bank[tsl][:, :]))
            sDVE.inc(v.tensor_copy(r_bc[:, sl], bank[4 + tsl][:, :]))
        M["p1_bcsb"] = sDVE.n
    DVE(p1_bcsb)

    xn_ready = {}
    for cb in range(NCB):
        def p1_norm(v, cb=cb):
            sl = slice(cb * T, (cb + 1) * T)
            if cb == 0:
                v.wait_ge(sDVE.h, M["p1_bcsb"])
            else:
                v.wait_ge(sDVE.h, xn_ready[cb - 1])
            sDVE.inc(v.tensor_sub(xn[:, sl], xn[:, sl], m_bc[:, :]))
            v.wait_ge(sDVE.h, sDVE.n)
            sDVE.inc(v.tensor_mul(xn[:, sl], xn[:, sl], r_bc[:, :]))
            xn_ready[cb] = sDVE.n
        DVE(p1_norm)

        def p1_spill(sy, cb=cb):
            sy.wait_ge(sDVE.h, xn_ready[cb])
            dSP.inc(sy.dma_start(xnscr[cb * 128:(cb + 1) * 128, :], xn[:, cb * T:(cb + 1) * T]), 16)
        SP(p1_spill)
    M["p1_xn"] = sDVE.n
    M["p1_spill"] = dSP.n

    def _fin(sy):
        for s_ in ALL_SEMS:
            if s_.n:
                sy.wait_ge(s_.h, s_.n)
    if upto <= 1:
        SP(_fin)
        return nc

    g1.close()
    g2 = ExitStack()
    cos_sb = g2.enter_context(nc.sbuf_tensor("cos_sb", [128, T], F32))
    sin_sb = g2.enter_context(nc.sbuf_tensor("sin_sb", [128, T], F32))
    wband = [g2.enter_context(nc.sbuf_tensor(f"wband{i}", [128, NCB * 128], BF16))
             for i in range(2)]
    stg = [g2.enter_context(nc.sbuf_tensor(f"stg{i}", [128, 512], BF16)) for i in range(4)]
    sqst = [g2.enter_context(nc.sbuf_tensor(f"sqst{i}", [128, 512], BF16)) for i in range(4)]
    t1_sb = g2.enter_context(nc.sbuf_tensor("t1_sb", [128, 512], F32))
    t2_sb = g2.enter_context(nc.sbuf_tensor("t2_sb", [128, 512], F32))

    def p2_consts(g):
        g.wait_ge(sDVE.h, M["p1_xn"])
        dC2.inc(g.dma_start(cos_sb[:], cosd[:, :]), 16)
        dC2.inc(g.dma_start(sin_sb[:], sind[:, :]), 16)
        M["p2_consts"] = dC2.n
    POOL(p2_consts)

    wld, wcons = {}, {}
    grp = {}
    evac = {}
    store = {}
    sqm = {}
    nm_done = {}
    nrm_read = {}
    adcopy = {}
    avtr = {}
    avcp = {}

    def p2_wload(g, og):
        if og >= 2:
            g.wait_ge(sPE.h, grp[(og - 2, NTS - 1)])
        s_ = dWB[og % 2]
        s_.inc(g.dma_start(
            bass.AP(wband[og % 2], 0, [[NCB * 128, 128], [128, NCB], [1, 128]]),
            bass.AP(wqkvT, og * 128, [[QKVO, 128], [128 * QKVO, NCB], [1, 128]])), 16)
        wld[og] = (s_.h, s_.n)

    POOL(lambda g: (p2_wload(g, 0), p2_wload(g, 1)))

    last_nrm_idx = [None]

    def emit_norm_lagged(idx):
        og, tsl = divmod(idx, NTS)
        if og >= 32:
            return

        def pe_nm(t):
            t.wait_ge(sACT.h, sqm[idx])
            if last_nrm_idx[0] is not None:
                t.wait_ge(sDVE.h, nrm_read[last_nrm_idx[0]])
            sPE.inc(t.matmul(Ek[0:1, 0:512], ones_bf[:, 0:1], sqst[idx % 4][:, :],
                             start=True, stop=True))
            nm_done[idx] = sPE.n
        PE(pe_nm)

        def dve_nr(v):
            v.wait_ge(sPE.h, nm_done[idx])
            sDVE.inc(v.reduce_max(scr1[0:1, 20:21], Ek[0:1, 0:512], axis=AX.X))
            v.wait_ge(sDVE.h, sDVE.n)
            sDVE.inc(v.tensor_max(nrm_sb[0:1, og:og + 1], nrm_sb[0:1, og:og + 1],
                                  scr1[0:1, 20:21]))
            nrm_read[idx] = sDVE.n
        DVE(dve_nr)
        last_nrm_idx[0] = idx

    for og in range(NOG):
        for tsl in range(NTS):
            idx = og * NTS + tsl

            def p2_grp(t, og=og, tsl=tsl, idx=idx):
                if tsl == 0:
                    t.wait_ge(*wld[og])
                    if og == 0:
                        t.wait_ge(sDVE.h, M["p1_xn"])
                    if og > 16 and (og - 1) >= 16:
                        t.wait_ge(sDVE.h, adcopy[og - 1])
                if idx >= 3:
                    t.wait_ge(*evac[idx - 3])
                rb = bank[idx % 3]
                ins = None
                for cb in range(NCB):
                    ins = t.matmul(rb[:, :], wband[og % 2][:, cb * 128:(cb + 1) * 128],
                                   xn[:, cb * T + tsl * 512:cb * T + (tsl + 1) * 512],
                                   start=(cb == 0), stop=(cb == NCB - 1))
                    if tsl == 0 and og >= 16:
                        ins = t.matmul(Dk[:, 0:10], wband[og % 2][:, cb * 128:(cb + 1) * 128],
                                       adT_sb[:, cb * 10:(cb + 1) * 10],
                                       start=(cb == 0), stop=(cb == NCB - 1))
                sPE.inc(ins)
                grp[(og, tsl)] = sPE.n
            PE(p2_grp)

            if og < 32:
                def p2_rope(v, og=og, tsl=tsl, idx=idx):
                    v.wait_ge(sPE.h, grp[(og, tsl)])
                    if idx == 0:
                        v.wait_ge(dC2.h, M["p2_consts"])
                    if idx >= 4:
                        v.wait_ge(*store[idx - 4])
                        if (idx - 4) // NTS < 32:
                            v.wait_ge(sACT.h, sqm[idx - 4])
                    if idx >= 1 and (idx - 1) in evac and evac[idx - 1][0] is sDVE.h:
                        v.wait_ge(*evac[idx - 1])
                    rb = bank[idx % 3]
                    sl = slice(tsl * 512, (tsl + 1) * 512)
                    sDVE.inc(v.tensor_mul(t1_sb[:, :], rb[:, :], cos_sb[:, sl]))
                    sDVE.inc(v.tensor_mul(t2_sb[0:64, :], rb[64:128, :], sin_sb[0:64, sl]))
                    sDVE.inc(v.tensor_mul(t2_sb[64:128, :], rb[0:64, :], sin_sb[64:128, sl]))
                    v.wait_ge(sDVE.h, sDVE.n)
                    sDVE.inc(v.tensor_add(stg[idx % 4][:, :], t1_sb[:, :], t2_sb[:, :]))
                    evac[idx] = (sDVE.h, sDVE.n)
                DVE(p2_rope)

                pass
            else:
                def p2_vcp(v, og=og, tsl=tsl, idx=idx):
                    v.wait_ge(sPE.h, grp[(og, tsl)])
                    if idx >= 4:
                        v.wait_ge(*store[idx - 4])
                    sDVE.inc(v.tensor_copy(stg[idx % 4][:, :], bank[idx % 3][:, :]))
                    evac[idx] = (sDVE.h, sDVE.n)
                DVE(p2_vcp)

            def p2_store(sy, og=og, tsl=tsl, idx=idx):
                sy.wait_ge(*evac[idx])
                tgt = qscr if og < 16 else (kscr if og < 32 else vscr)
                row = (og % 16) * 128
                s_ = dST[idx % 4]
                s_.inc(sy.dma_start(tgt[row:row + 128, tsl * 512:(tsl + 1) * 512],
                                    stg[idx % 4][:, :]), 16)
                store[idx] = (s_.h, s_.n)
            SP(p2_store)



            if tsl == 0 and og >= 16:
                def p2_adcp(v, og=og):
                    v.wait_ge(sPE.h, grp[(og, 0)])
                    if og >= 32 and og - 1 >= 32:
                        v.wait_ge(sPE.h, avtr[og - 1])
                    if og < 32:
                        ins = v.tensor_copy(ak_sb[:, (og - 16) * 10:(og - 15) * 10], Dk[:, 0:10])
                    else:
                        ins = v.tensor_copy(avtmp[:, :], Dk[:, 0:10])
                    sDVE.inc(ins)
                    adcopy[og] = sDVE.n
                DVE(p2_adcp)

            if tsl == 1 and og >= 32:
                def p2_avtr(t, og=og):
                    t.wait_ge(sDVE.h, adcopy[og])
                    if og - 1 >= 32:
                        t.wait_ge(sDVE.h, avcp[og - 1])
                    sPE.inc(t.transpose(Fk[0:10, 0:128], avtmp[:, :], ident_sb[:, :]))
                    avtr[og] = sPE.n
                PE(p2_avtr)

                def p2_avcp(v, og=og):
                    v.wait_ge(sPE.h, avtr[og])
                    sDVE.inc(v.tensor_copy(av_sb[:, (og - 32) * 128:(og - 31) * 128],
                                           Fk[0:10, 0:128]))
                    avcp[og] = sDVE.n
                DVE(p2_avcp)

        if og >= 1 and og - 1 < 32:
            for t2 in range(NTS):
                emit_norm_lagged((og - 1) * NTS + t2)
        if og < 32:
            def p2_sq4(s, og=og):
                for tsl in range(NTS):
                    idx = og * NTS + tsl
                    s.wait_ge(*evac[idx])
                    if idx >= 4 and (idx - 4) // NTS < 32:
                        s.wait_ge(sPE.h, nm_done[idx - 4])
                    sACT.inc(s.activation(sqst[idx % 4][:, :], stg[idx % 4][:, :], AF.Square))
                    sqm[idx] = sACT.n
            ACT(p2_sq4)
        if og + 2 < NOG:
            POOL(lambda g, og=og: p2_wload(g, og + 2))

    for t2 in range(NTS):
        emit_norm_lagged(31 * NTS + t2)
    M["p2_pe"] = sPE.n
    M["p2_store"] = [(s_.h, s_.n) for s_ in dST]

    def p2_aksq(s):
        s.wait_ge(sDVE.h, adcopy[31])
        sACT.inc(s.activation(aksq_sb[:, :], ak_sb[:, :], AF.Square))
        M["p2_aksq"] = sACT.n
    ACT(p2_aksq)

    def p2_aknm(t):
        t.wait_ge(sACT.h, M["p2_aksq"])
        t.wait_ge(sDVE.h, nrm_read[last_nrm_idx[0]])
        sPE.inc(t.matmul(Ek[0:1, 0:160], ones_bf[:, 0:1], aksq_sb[:, :], start=True, stop=True))
        M["p2_aknm"] = sPE.n
    PE(p2_aknm)

    def p2_akred(v):
        v.wait_ge(sPE.h, M["p2_aknm"])
        for h in range(NHEADS):
            sDVE.inc(v.reduce_max(nrm_sb[0:1, 32 + h:33 + h], Ek[0:1, h * 10:(h + 1) * 10],
                                  axis=AX.X))
        M["p2_akred"] = sDVE.n
    DVE(p2_akred)

    def p2_ln(s):
        s.wait_ge(sDVE.h, M["p2_akred"])
        sACT.inc(s.activation(scr1[0:1, 32:48], nrm_sb[0:1, 32:48], AF.Ln, bias=np.float32(1e-30)))
        sACT.inc(s.activation(scr1[0:1, 0:32], nrm_sb[0:1, 0:32], AF.Ln, bias=np.float32(1e-30)))
        M["p2_ln"] = sACT.n
    ACT(p2_ln)

    def p2_lns(v):
        v.wait_ge(sACT.h, M["p2_ln"])
        sDVE.inc(v.tensor_add(scr1[0:1, 48:64], scr1[0:1, 0:16], scr1[0:1, 16:32]))
        sDVE.inc(v.tensor_add(scr1[0:1, 16:32], scr1[0:1, 0:16], scr1[0:1, 32:48]))
        M["p2_lns"] = sDVE.n
    DVE(p2_lns)

    def p2_exp(s):
        s.wait_ge(sDVE.h, M["p2_lns"])
        sACT.inc(s.activation(scr1[0:1, 48:64], scr1[0:1, 48:64], AF.Exp, scale=np.float32(0.5)))
        sACT.inc(s.activation(scr1[0:1, 16:32], scr1[0:1, 16:32], AF.Exp, scale=np.float32(0.5)))
        M["p2_exp"] = sACT.n
    ACT(p2_exp)

    def p2_nbh(v):
        v.wait_ge(sACT.h, M["p2_exp"])
        sDVE.inc(v.tensor_scalar_mul(scr1[0:1, 48:64], scr1[0:1, 48:64], -S_SCALE * 1.0002))
        sDVE.inc(v.tensor_scalar_mul(scr1[0:1, 16:32], scr1[0:1, 16:32], -A_SCALE * 1.0002))
        M["p2_nbh"] = sDVE.n
    DVE(p2_nbh)

    def p2_nbhbc(t):
        t.wait_ge(sDVE.h, M["p2_nbh"])
        t.matmul(Gk[:, 0:16], ones1_sb[0:1, 0:128], scr1[0:1, 48:64], start=True, stop=True)
        sPE.inc(t.matmul(Gk[0:10, 16:32], ones1_sb[0:1, 0:10], scr1[0:1, 16:32],
                         start=True, stop=True))
        M["p2_nbhbc"] = sPE.n
    PE(p2_nbhbc)

    def p2_nbhsb(v):
        v.wait_ge(sPE.h, M["p2_nbhbc"])
        sDVE.inc(v.tensor_copy(nbh_bc[:, :], Gk[:, 0:16]))
        sDVE.inc(v.tensor_copy(nbha_bc[:, :], Gk[0:10, 16:32]))
        M["p2_nbhsb"] = sDVE.n
    DVE(p2_nbhsb)

    M["p2_act"] = sACT.n
    M["p2_dve"] = sDVE.n

    def _fin(sy):
        for s_ in ALL_SEMS:
            if s_.n:
                sy.wait_ge(s_.h, s_.n)
    if upto <= 2:
        SP(_fin)
        return nc
    g2.close()
    g12.close()

    g3 = ExitStack()
    k_fm = g3.enter_context(nc.sbuf_tensor("k_fm", [128, NHEADS * T], BF16))
    v_tm = g3.enter_context(nc.sbuf_tensor("v_tm", [128, 16 * CH], BF16))
    at_sb = g3.enter_context(nc.sbuf_tensor("at_sb", [128, 2 * 16 * 512], BF16))
    ystg = [g3.enter_context(nc.sbuf_tensor(f"ystg{i}", [128, 512], BF16)) for i in range(2)]
    q_sb = [g3.enter_context(nc.sbuf_tensor(f"q_sb{i}", [128, 512], BF16)) for i in range(2)]
    expaT = g3.enter_context(nc.sbuf_tensor("expaT", [10, 512], BF16))
    expaS = g3.enter_context(nc.sbuf_tensor("expaS", [10, 512], BF16))
    abcs = g3.enter_context(nc.sbuf_tensor("abcs", [10, 512], F32))
    ra_row = g3.enter_context(nc.sbuf_tensor("ra_row", [1, 512], F32))
    recm = g3.enter_context(nc.sbuf_tensor("recm", [128, 8], F32))

    def atsl(slot, c):
        return at_sb[:, (slot * 16 + c) * 512:(slot * 16 + c) * 512 + 512]

    kvld = {}
    qld = {}

    def p3_kv(sy):
        for mk in M["p2_store"]:
            sy.wait_ge(*mk)
        sy.wait_ge(sPE.h, M["p2_pe"])
        sy.wait_ge(sACT.h, M["p2_act"])
        sy.wait_ge(sDVE.h, M["p2_dve"])
        for h in range(NHEADS):
            dKV.inc(sy.dma_start(k_fm[:, h * T:(h + 1) * T], kscr[h * 128:(h + 1) * 128, :]), 16)
        for tc in range(16):
            dKV.inc(sy.dma_start(v_tm[:, tc * CH:(tc + 1) * CH],
                                 vscr[0:CH, tc * 128:(tc + 1) * 128], transpose=True), 16)
        M["p3_kv"] = dKV.n
        for gen in (0, 1):
            h, sl = divmod(gen, NTS)
            s_ = dQ[gen % 2]
            s_.inc(sy.dma_start(q_sb[gen % 2][:, :],
                                qscr[h * 128:(h + 1) * 128, sl * 512:(sl + 1) * 512]), 16)
            qld[gen] = (s_.h, s_.n)
    SP(p3_kv)

    expm = {}
    maskm = {}
    scorem = {}
    bank_free = [(sACT.h, 0), (sACT.h, 0)]
    sums_done = {}
    y_done = {}
    ascm = {}
    aexpm = {}
    asumm = {}
    arrm = {}
    abcm = {}
    adwm = {}
    recmm = {}
    evacm = {}
    yaddm = {}
    ystorem = {}
    sums_free = {0: (sDVE.h, 0), 1: (sDVE.h, 0)}

    act_last_bank = [0, 0]
    for h in range(NHEADS):
        for sl in range(NTS):
            gen = h * NTS + sl
            nch = 4 * sl + 4
            act_base = sACT.n

            def p3_peA(t, h=h, gen=gen, nch=nch, ab=act_base,
                       lb=tuple(act_last_bank)):
                t.wait_ge(*qld[gen])
                if gen == 0:
                    t.wait_ge(dKV.h, M["p3_kv"])
                    t.wait_ge(sDVE.h, M["p2_nbhsb"])
                for c in range(nch):
                    if c >= 2:
                        t.wait_ge(sACT.h, ab + (c - 2) + 1)
                    elif lb[c % 2]:
                        t.wait_ge(sACT.h, lb[c % 2])
                    sPE.inc(t.matmul(bank[c % 2][:, :],
                                     k_fm[:, h * T + c * 128:h * T + (c + 1) * 128],
                                     q_sb[gen % 2][:, :], start=True, stop=True))
                    scorem[(gen, c)] = sPE.n
                if gen >= 1:
                    t.wait_ge(sDVE.h, adwm[gen - 1])
                sPE.inc(t.matmul(Gk[0:10, 0:512], ak_sb[:, h * 10:(h + 1) * 10],
                                 q_sb[gen % 2][:, :], start=True, stop=True))
                ascm[gen] = sPE.n
            PE(p3_peA)
            for c in range(nch):
                act_last_bank[c % 2] = act_base + c + 1

            def p3_act(s, h=h, gen=gen, nch=nch, ab=act_base):
                for c in range(nch):
                    s.wait_ge(sPE.h, scorem[(gen, c)])
                    if c == 0 and gen >= 2:
                        s.wait_ge(sPE.h, y_done[gen - 2])
                    sACT.inc(s.activation(atsl(gen % 2, c), bank[c % 2][:, :], AF.Exp,
                                          bias=nbh_bc[:, h:h + 1], scale=np.float32(S_SCALE)))
                    assert sACT.n == ab + c + 1
                    expm[(gen, c)] = sACT.n
                s.wait_ge(sPE.h, ascm[gen])
                if gen >= 1:
                    s.wait_ge(sDVE.h, adwm[gen - 1])
                sACT.inc(s.activation(expaT[0:10, :], Gk[0:10, 0:512], AF.Exp,
                                      bias=nbha_bc[:, h:h + 1], scale=np.float32(A_SCALE)))
                aexpm[gen] = sACT.n
            ACT(p3_act)

            def p3_masks(v, gen=gen, sl=sl, nch=nch):
                for c in range(4 * sl, nch):
                    v.wait_ge(sACT.h, expm[(gen, c)])
                    r = c - 4 * sl
                    sDVE.inc(v.tensor_mul(atsl(gen % 2, c), atsl(gen % 2, c),
                                          mask_sb[:, r * 512:(r + 1) * 512]))
                    maskm[(gen, c)] = sDVE.n
            DVE(p3_masks)

            if gen + 2 < NHEADS * NTS:
                def p3_q(sy, gen=gen):
                    g2 = gen + 2
                    h2, sl2 = divmod(g2, NTS)
                    sy.wait_ge(sPE.h, ascm[gen])
                    s_ = dQ[g2 % 2]
                    s_.inc(sy.dma_start(q_sb[g2 % 2][:, :],
                                        qscr[h2 * 128:(h2 + 1) * 128,
                                             sl2 * 512:(sl2 + 1) * 512]), 16)
                    qld[g2] = (s_.h, s_.n)
                SP(p3_q)

            def p3_peB1(t, h=h, sl=sl, gen=gen, nch=nch):
                t.wait_ge(sACT.h, expm[(gen, nch - 1)])
                t.wait_ge(sDVE.h, maskm[(gen, nch - 1)])
                t.wait_ge(*sums_free[gen % 2])
                sb_ = bank[4 + (gen % 2)]
                for j in range(4):
                    if gen >= 1:
                        t.wait_ge(*evacm[(gen - 1, j)])
                    yb = bank[2 + (j // 2)]
                    yc = (j % 2) * 128
                    for c in range(nch):
                        lhs = at_sb[:, ((gen % 2) * 16 + c) * 512 + j * 128:
                                    ((gen % 2) * 16 + c) * 512 + (j + 1) * 128]
                        t.matmul(yb[:, yc:yc + 128], lhs,
                                 v_tm[:, c * CH + h * 128:c * CH + (h + 1) * 128],
                                 start=(c == 0), stop=False)
                        ins = t.matmul(sb_[:, j:j + 1], lhs, ones_bf[:, 0:1],
                                       start=(c == 0), stop=(c == nch - 1))
                    sPE.inc(ins)
                    sums_done[(gen, j)] = sPE.n
                t.wait_ge(sACT.h, aexpm[gen])
                sPE.inc(t.matmul(Gk[0:1, 0:512], ones_bf[0:10, 0:1], expaT[0:10, :],
                                 start=True, stop=True))
                asumm[gen] = sPE.n
            PE(p3_peB1)

            def p3_arr(v, gen=gen):
                v.wait_ge(sPE.h, asumm[gen])
                sDVE.inc(v.reciprocal(ra_row[0:1, :], Gk[0:1, 0:512]))
                v.wait_ge(sDVE.h, sDVE.n)
                sDVE.inc(v.tensor_single_scalar(ra_row[0:1, :], ra_row[0:1, :],
                                                gat_sb[0:1, 0:1], ALU.mult))
                arrm[gen] = sDVE.n
            DVE(p3_arr)

            def p3_abc(t, gen=gen):
                t.wait_ge(sDVE.h, arrm[gen])
                sPE.inc(t.matmul(Gk[0:10, 0:512], ones1_sb[0:1, 0:10], ra_row[0:1, :],
                                 start=True, stop=True))
                abcm[gen] = sPE.n
            PE(p3_abc)

            def p3_adw(v, gen=gen):
                v.wait_ge(sPE.h, abcm[gen])
                if gen >= 1:
                    v.wait_ge(sPE.h, y_done[gen - 1])
                sDVE.inc(v.tensor_copy(abcs[0:10, :], Gk[0:10, 0:512]))
                v.wait_ge(sDVE.h, sDVE.n)
                sDVE.inc(v.tensor_mul(expaS[0:10, :], expaT[0:10, :], abcs[0:10, :]))
                adwm[gen] = sDVE.n
            DVE(p3_adw)

            def p3_peB3(t, h=h, gen=gen):
                t.wait_ge(sDVE.h, adwm[gen])
                if gen >= 1:
                    t.wait_ge(sDVE.h, yaddm[gen - 1])
                ins = None
                for j in range(4):
                    ins = t.matmul(Hk[:, j * 128:(j + 1) * 128],
                                   expaS[0:10, j * 128:(j + 1) * 128],
                                   av_sb[0:10, h * 128:(h + 1) * 128],
                                   start=True, stop=True)
                sPE.inc(ins)
                y_done[gen] = sPE.n
            PE(p3_peB3)

            def p3_epi(v, gen=gen):
                v.wait_ge(sPE.h, sums_done[(gen, 3)])
                if gen >= 2:
                    v.wait_ge(*evacm[(gen - 2, 3)])
                    v.wait_ge(*ystorem[gen - 2])
                for j in range(4):
                    sDVE.inc(v.reciprocal(recm[:, (gen % 2) * 4 + j:(gen % 2) * 4 + j + 1],
                                          bank[4 + (gen % 2)][:, j:j + 1]))
                recmm[gen] = sDVE.n
                v.wait_ge(sDVE.h, sDVE.n)
                for j in range(4):
                    yb = bank[2 + (j // 2)]
                    yc = (j % 2) * 128
                    sDVE.inc(v.tensor_single_scalar(
                        ystg[gen % 2][:, j * 128:(j + 1) * 128], yb[:, yc:yc + 128],
                        recm[:, (gen % 2) * 4 + j:(gen % 2) * 4 + j + 1], ALU.mult))
                    evacm[(gen, j)] = (sDVE.h, sDVE.n)
                v.wait_ge(sPE.h, y_done[gen])
                v.wait_ge(sDVE.h, sDVE.n)
                sDVE.inc(v.tensor_add(ystg[gen % 2][:, :], ystg[gen % 2][:, :], Hk[:, 0:512]))
                yaddm[gen] = sDVE.n
            DVE(p3_epi)
            sums_free[gen % 2] = (sDVE.h, recmm[gen])

            def p3_yst(sy, h=h, sl=sl, gen=gen):
                sy.wait_ge(sDVE.h, yaddm[gen])
                s_ = dY[gen % 2]
                s_.inc(sy.dma_start(
                    bass.AP(yscr, sl * 512 * CH + h * 128, [[CH, 128], [128 * CH, 4], [1, 128]]),
                    bass.AP(ystg[gen % 2], 0, [[512, 128], [128, 4], [1, 128]])), 16)
                ystorem[gen] = (s_.h, s_.n)
            SP(p3_yst)

    LASTG = NHEADS * NTS - 1

    def p3_ag(g):
        g.wait_ge(dY[0].h, dY[0].n)
        g.wait_ge(dY[1].h, dY[1].n)
        sCC.inc(g.collective_compute(
            "AllGather", ALU.bypass,
            replica_groups=[[0, 1], [2, 3], [4, 5], [6, 7]],
            ins=[yscr.ap()], outs=[agy.ap()]))
        M["ag"] = sCC.n
        if debug:
            g.wait_ge(sCC.h, M["ag"])
            dDBG.inc(g.dma_start(agy_dbg[:, :], agy[:, :]), 16)
            g.wait_ge(dDBG.h, dDBG.n)
    POOL(p3_ag)

    M["p3_pe"] = sPE.n
    M["p3_act"] = sACT.n
    M["p3_dve"] = sDVE.n

    def _fin(sy):
        for s_ in ALL_SEMS:
            if s_.n:
                sy.wait_ge(s_.h, s_.n)
    if upto <= 3:
        SP(_fin)
        return nc
    g3.close()

    g4 = ExitStack()
    xn2 = g4.enter_context(nc.sbuf_tensor("xn2", [128, NCB * T], BF16))
    wband4 = [g4.enter_context(nc.sbuf_tensor(f"wband4_{i}", [128, NCB * 128], BF16))
              for i in range(2)]
    mrT_sb = g4.enter_context(nc.sbuf_tensor("mrT_sb", [48, T], BF16))
    xstg = [g4.enter_context(nc.sbuf_tensor(f"xstg{i}", [128, 512], F32)) for i in range(2)]
    sstg = [g4.enter_context(nc.sbuf_tensor(f"sstg{i}", [128, 512], F32)) for i in range(2)]

    xn2ld = {}

    def p4_xn(sy):
        sy.wait_ge(sPE.h, M["p3_pe"])
        sy.wait_ge(sACT.h, M["p3_act"])
        sy.wait_ge(sDVE.h, M["p3_dve"])
        sy.wait_ge(dSP.h, M["p1_spill"])
        for cb in range(NCB):
            dXN2.inc(sy.dma_start(xn2[:, cb * T:(cb + 1) * T], xnscr[cb * 128:(cb + 1) * 128, :]), 16)
        M["p4_xn"] = dXN2.n
    SP(p4_xn)

    w4ld, w4grp = {}, {}

    def p4_wload(g, og):
        if og >= 2:
            g.wait_ge(sPE.h, w4grp[(og - 2, NTS - 1)])
        s_ = dW4[og % 2]
        s_.inc(g.dma_start(
            bass.AP(wband4[og % 2], 0, [[NCB * 128, 128], [128, NCB], [1, 128]]),
            bass.AP(wffnT, og * 128, [[CH, 128], [128 * CH, NCB], [1, 128]])), 16)
        w4ld[og] = (s_.h, s_.n)

    POOL(lambda g: (g.wait_ge(sPE.h, M["p3_pe"]), p4_wload(g, 0), p4_wload(g, 1)))

    mrm = {}
    for tsl in range(NTS):
        def p4_mr(t, tsl=tsl):
            if tsl == 0:
                t.wait_ge(dXN2.h, M["p4_xn"])
            else:
                t.wait_ge(sDVE.h, mrm[("s", tsl - 1)])
            ins = None
            for cb in range(NCB):
                ins = t.matmul(Dk[0:48, 0:512], loraA_sb[:, cb * 48:(cb + 1) * 48],
                               xn2[:, cb * T + tsl * 512:cb * T + (tsl + 1) * 512],
                               start=(cb == 0), stop=(cb == NCB - 1))
            sPE.inc(ins)
            mrm[("g", tsl)] = sPE.n
        PE(p4_mr)

        def p4_mrs(v, tsl=tsl):
            v.wait_ge(sPE.h, mrm[("g", tsl)])
            sDVE.inc(v.tensor_single_scalar(mrT_sb[0:48, tsl * 512:(tsl + 1) * 512],
                                            Dk[0:48, 0:512], m48_sb[0:48, 0:1], ALU.mult))
            mrm[("s", tsl)] = sDVE.n
        DVE(p4_mrs)

    fev = {}
    fst = {}
    xres = {}
    for og in range(16):
        for tsl in range(NTS):
            idx = og * NTS + tsl

            def p4_xres(sy, og=og, tsl=tsl, idx=idx):
                if idx >= 2:
                    sy.wait_ge(sDVE.h, fev[idx - 2])
                s_ = dXR[idx % 2]
                s_.inc(sy.dma_start(xstg[idx % 2][:, :],
                                    xT[og * 128:(og + 1) * 128, tsl * 512:(tsl + 1) * 512]), 16)
                xres[idx] = (s_.h, s_.n)
            SP(p4_xres)

            def p4_grp(t, og=og, tsl=tsl, idx=idx):
                if tsl == 0:
                    t.wait_ge(*w4ld[og])
                if idx == 0:
                    t.wait_ge(sDVE.h, mrm[("s", NTS - 1)])
                if idx >= 3:
                    t.wait_ge(sDVE.h, fev[idx - 3])
                rb = bank[idx % 3]
                for cb in range(NCB):
                    t.matmul(rb[:, :], wband4[og % 2][:, cb * 128:(cb + 1) * 128],
                             xn2[:, cb * T + tsl * 512:cb * T + (tsl + 1) * 512],
                             start=(cb == 0), stop=False)
                ins = t.matmul(rb[:, :], loraB_sb[0:48, og * 128:(og + 1) * 128],
                               mrT_sb[0:48, tsl * 512:(tsl + 1) * 512],
                               start=False, stop=True)
                sPE.inc(ins)
                w4grp[(og, tsl)] = sPE.n
            PE(p4_grp)

            def p4_ev(v, og=og, tsl=tsl, idx=idx):
                v.wait_ge(sPE.h, w4grp[(og, tsl)])
                v.wait_ge(*xres[idx])
                if idx >= 2:
                    v.wait_ge(*fst[idx - 2])
                sDVE.inc(v.scalar_tensor_tensor(sstg[idx % 2][:, :], bank[idx % 3][:, :],
                                                biasT_sb[:, og:og + 1], xstg[idx % 2][:, :],
                                                ALU.add, ALU.add))
                fev[idx] = sDVE.n
            DVE(p4_ev)

            def p4_st(sy, og=og, tsl=tsl, idx=idx):
                sy.wait_ge(sDVE.h, fev[idx])
                s_ = dFS[idx % 2]
                s_.inc(sy.dma_start(fspill[og * 128:(og + 1) * 128, tsl * 512:(tsl + 1) * 512],
                                    sstg[idx % 2][:, :]), 16)
                fst[idx] = (s_.h, s_.n)
            SP(p4_st)

        if og + 2 < 16:
            POOL(lambda g, og=og: p4_wload(g, og + 2))

    M["p4_pe"] = sPE.n
    M["p4_act"] = sACT.n
    M["p4_dve"] = sDVE.n
    M["p4_fst"] = [(s_.h, s_.n) for s_ in dFS]

    def _fin(sy):
        for s_ in ALL_SEMS:
            if s_.n:
                sy.wait_ge(s_.h, s_.n)
    if upto <= 4:
        SP(_fin)
        return nc
    g4.close()

    g5 = ExitStack()
    yfull = g5.enter_context(nc.sbuf_tensor("yfull", [128, NCB * T], BF16))
    wband5 = [g5.enter_context(nc.sbuf_tensor(f"wband5_{i}", [128, NCB * 128], BF16))
              for i in range(2)]
    ostg = [g5.enter_context(nc.sbuf_tensor(f"ostg{i}", [128, 512], F32)) for i in range(2)]
    pstg = [g5.enter_context(nc.sbuf_tensor(f"pstg{i}", [128, 512], F32)) for i in range(2)]

    yfld = {}

    def p5_y(sy):
        sy.wait_ge(sCC.h, M["ag"])
        sy.wait_ge(sPE.h, M["p4_pe"])
        sy.wait_ge(sDVE.h, M["p4_dve"])
        for mk in M["p4_fst"]:
            sy.wait_ge(*mk)
        for cc2 in range(NCB):
            rank = cc2 // 16
            col = (cc2 % 16) * 128
            dYF.inc(sy.dma_start(yfull[:, cc2 * T:(cc2 + 1) * T],
                                 agy[rank * T:(rank + 1) * T, col:col + 128], transpose=True), 16)
        M["p5_yf"] = dYF.n
    SP(p5_y)

    w5ld, w5grp = {}, {}

    def p5_wload(g, og):
        if og >= 2:
            g.wait_ge(sPE.h, w5grp[(og - 2, NTS - 1)])
        s_ = dW5[og % 2]
        s_.inc(g.dma_start(
            bass.AP(wband5[og % 2], 0, [[NCB * 128, 128], [128, NCB], [1, 128]]),
            bass.AP(wprojT, og * 128, [[CH, 128], [128 * CH, NCB], [1, 128]])), 16)
        w5ld[og] = (s_.h, s_.n)

    POOL(lambda g: (g.wait_ge(sPE.h, M["p4_pe"]), p5_wload(g, 0), p5_wload(g, 1)))

    oev = {}
    ost = {}
    pld = {}
    for og in range(16):
        for tsl in range(NTS):
            idx = og * NTS + tsl

            def p5_pld(sy, og=og, tsl=tsl, idx=idx):
                if idx >= 2:
                    sy.wait_ge(sDVE.h, oev[idx - 2])
                s_ = dPL[idx % 2]
                s_.inc(sy.dma_start(pstg[idx % 2][:, :],
                                    fspill[og * 128:(og + 1) * 128, tsl * 512:(tsl + 1) * 512]), 16)
                pld[idx] = (s_.h, s_.n)
            SP(p5_pld)

            def p5_grp(t, og=og, tsl=tsl, idx=idx):
                if tsl == 0:
                    t.wait_ge(*w5ld[og])
                if idx == 0:
                    t.wait_ge(dYF.h, M["p5_yf"])
                if idx >= 3:
                    t.wait_ge(sDVE.h, oev[idx - 3])
                rb = bank[idx % 3]
                ins = None
                for cb in range(NCB):
                    ins = t.matmul(rb[:, :], wband5[og % 2][:, cb * 128:(cb + 1) * 128],
                                   yfull[:, cb * T + tsl * 512:cb * T + (tsl + 1) * 512],
                                   start=(cb == 0), stop=(cb == NCB - 1))
                sPE.inc(ins)
                w5grp[(og, tsl)] = sPE.n
            PE(p5_grp)

            def p5_ev(v, og=og, tsl=tsl, idx=idx):
                v.wait_ge(sPE.h, w5grp[(og, tsl)])
                v.wait_ge(*pld[idx])
                if idx >= 2:
                    v.wait_ge(*ost[idx - 2])
                sDVE.inc(v.tensor_add(ostg[idx % 2][:, :], bank[idx % 3][:, :],
                                      pstg[idx % 2][:, :]))
                oev[idx] = sDVE.n
            DVE(p5_ev)

            def p5_st(sy, og=og, tsl=tsl, idx=idx):
                sy.wait_ge(sDVE.h, oev[idx])
                s_ = dOUT[idx % 2]
                s_.inc(sy.dma_start(outT[og * 128:(og + 1) * 128, tsl * 512:(tsl + 1) * 512],
                                    ostg[idx % 2][:, :]), 16)
                ost[idx] = (s_.h, s_.n)
            SP(p5_st)

        if og + 2 < 16:
            POOL(lambda g, og=og: p5_wload(g, og + 2))

    SP(lambda sy: (sy.wait_ge(dOUT[0].h, dOUT[0].n), sy.wait_ge(dOUT[1].h, dOUT[1].n)))

    return nc


_CACHE = {}


def _in_maps(inputs):
    x = np.ascontiguousarray(np.asarray(inputs["L0_attn_x"], dtype=np.float32))
    w_attn = np.asarray(inputs["L0_attn_w_attn"], dtype=np.float32)
    w_proj = np.asarray(inputs["L0_attn_w_proj"], dtype=np.float32)
    adapter = np.asarray(inputs["L0_attn_adapter_wte"], dtype=np.float32)
    gating = np.asarray(inputs["L0_attn_gating"], dtype=np.float32)
    ffn_w = np.asarray(inputs["L0_ffn_weight"], dtype=np.float32)
    bias = np.asarray(inputs["L0_ffn_bias"], dtype=np.float32)
    lora_A = np.asarray(inputs["L0_ffn_lora_A"], dtype=np.float32)
    lora_B = np.asarray(inputs["L0_ffn_lora_B"], dtype=np.float32)
    lsi = np.asarray(inputs["L0_ffn_lora_start_indices"]).astype(np.int32)

    cosfm, sinfm, masks_bf, ident, ones1, a48 = _host_consts()
    wqkvT_full = np.ascontiguousarray(w_attn.T)
    wprojT_full = np.ascontiguousarray(w_proj.T)
    wffnT_full = np.ascontiguousarray(ffn_w.T)
    loraAT_full = np.ascontiguousarray(np.transpose(lora_A, (2, 0, 1)).reshape(C, 48))
    loraBT_full = np.ascontiguousarray(np.transpose(lora_B, (0, 2, 1)).reshape(48, C))
    adT_full = np.ascontiguousarray(adapter.T)

    maps = []
    for c in range(8):
        b, g = c // 2, c % 2
        Fg = slice(g * CH, (g + 1) * CH)
        perm = np.r_[g * CH:(g + 1) * CH, (1 - g) * CH:(2 - g) * CH]
        qcols = np.r_[g * CH:(g + 1) * CH, C + g * CH:C + (g + 1) * CH,
                      2 * C + g * CH:2 * C + (g + 1) * CH]
        maps.append({
            "xT": np.ascontiguousarray(x[b].T[perm]),
            "wqkvT": np.ascontiguousarray(wqkvT_full[perm][:, qcols]),
            "wprojT": np.ascontiguousarray(wprojT_full[:, Fg]),
            "wffnT": np.ascontiguousarray(wffnT_full[perm][:, Fg]),
            "biasT": np.ascontiguousarray(bias[Fg].reshape(16, 128).T),
            "loraAT": np.ascontiguousarray(loraAT_full[perm]),
            "loraBT": np.ascontiguousarray(loraBT_full[:, Fg]),
            "adT": np.ascontiguousarray(adT_full[perm]),
            "gat": gating.reshape(1, 1),
            "lsi": lsi.reshape(1, 4),
            "bvec": np.array([[float(b)]], np.float32),
            "cosd": cosfm, "sind": sinfm, "maskd": masks_bf,
            "identd": ident, "ones1d": ones1, "a48d": a48,
            "onesbd": np.ones((128, 1), ml_dtypes.bfloat16),
            "zerod": np.zeros((1, 64), np.float32),
        })
    return maps


def kernel(**inputs):
    maps = _in_maps(inputs)
    if "nc" not in _CACHE:
        _CACHE["nc"] = build_nc()
    res = run_bass_kernel_spmd(_CACHE["nc"], maps, core_ids=list(range(8)))
    out = np.empty((B, T, C), np.float32)
    for c in range(8):
        b, g = c // 2, c % 2
        out[b, :, g * CH:(g + 1) * CH] = res.results[c]["outT"].T
    return out


# version 03 (final)
# speedup vs baseline: 1.0165x; 1.0165x over previous
import numpy as np
import ml_dtypes
from contextlib import ExitStack
import concourse.bass as bass
import concourse.mybir as mybir
from concourse.bass_utils import run_bass_kernel_spmd

F32 = mybir.dt.float32
BF16 = mybir.dt.bfloat16
I32 = mybir.dt.int32
AF = mybir.ActivationFunctionType
ALU = mybir.AluOpType
AX = mybir.AxisListType

B, T, C, H, HD = 4, 2048, 4096, 32, 128
NHEADS = 16
CH = C // 2
NCB = C // 128
NTS = T // 512
QKVO = 3 * CH
NOG = QKVO // 128
S_SCALE = float(1.0 / np.sqrt(HD))
A_SCALE = float(1.0 / np.sqrt(C))
EPS = 1e-5


def _host_consts():
    th = 10000.0 ** (-np.arange(0, HD, 2, dtype=np.float64) / HD)
    t = np.arange(T, dtype=np.float64)
    ang = np.outer(th, t)
    cosfm = np.concatenate([np.cos(ang), np.cos(ang)], 0).astype(np.float32)
    sinfm = np.concatenate([-np.sin(ang), np.sin(ang)], 0).astype(np.float32)
    masks = np.zeros((128, 4 * 512), np.float32)
    ii = np.arange(128)[:, None]
    jj = np.arange(512)[None, :]
    for r in range(4):
        masks[:, r * 512:(r + 1) * 512] = (128 * r + ii <= jj)
    ident = np.eye(128, dtype=np.float32)
    ones1 = np.ones((1, 128), np.float32)
    a48 = (np.arange(48) // 16).astype(np.float32).reshape(48, 1)
    return (cosfm, sinfm, masks.astype(ml_dtypes.bfloat16), ident, ones1, a48)


def build_nc(debug=False, upto=9):
    nc = bass.Bass("TRN2", target_bir_lowering=False)
    scratch_kind = "ExternalOutput" if debug else "Internal"

    xT = nc.dram_tensor("xT", [C, T], F32, kind="ExternalInput")
    wqkvT = nc.dram_tensor("wqkvT", [C, QKVO], F32, kind="ExternalInput")
    wprojT = nc.dram_tensor("wprojT", [C, CH], F32, kind="ExternalInput")
    wffnT = nc.dram_tensor("wffnT", [C, CH], F32, kind="ExternalInput")
    biasT = nc.dram_tensor("biasT", [128, 16], F32, kind="ExternalInput")
    loraAT = nc.dram_tensor("loraAT", [C, 48], F32, kind="ExternalInput")
    loraBT = nc.dram_tensor("loraBT", [48, CH], F32, kind="ExternalInput")
    adT = nc.dram_tensor("adT", [C, 10], F32, kind="ExternalInput")
    gat = nc.dram_tensor("gat", [1, 1], F32, kind="ExternalInput")
    lsi = nc.dram_tensor("lsi", [1, 4], I32, kind="ExternalInput")
    bvec = nc.dram_tensor("bvec", [1, 1], F32, kind="ExternalInput")
    cosd = nc.dram_tensor("cosd", [128, T], F32, kind="ExternalInput")
    sind = nc.dram_tensor("sind", [128, T], F32, kind="ExternalInput")
    maskd = nc.dram_tensor("maskd", [128, 2048], BF16, kind="ExternalInput")
    identd = nc.dram_tensor("identd", [128, 128], F32, kind="ExternalInput")
    ones1d = nc.dram_tensor("ones1d", [1, 128], F32, kind="ExternalInput")
    a48d = nc.dram_tensor("a48d", [48, 1], F32, kind="ExternalInput")
    onesbd = nc.dram_tensor("onesbd", [128, 1], BF16, kind="ExternalInput")
    zerod = nc.dram_tensor("zerod", [1, 64], F32, kind="ExternalInput")

    outT = nc.dram_tensor("outT", [CH, T], F32, kind="ExternalOutput")

    xnscr = nc.dram_tensor("xnscr", [C, T], BF16, kind=scratch_kind)
    qscr = nc.dram_tensor("qscr", [CH, T], BF16, kind=scratch_kind)
    kscr = nc.dram_tensor("kscr", [CH, T], BF16, kind=scratch_kind)
    vscr = nc.dram_tensor("vscr", [CH, T], BF16, kind=scratch_kind)
    yscr = nc.dram_tensor("yscr", [T, CH], BF16)
    agy = nc.dram_tensor("agy", [2 * T, CH], BF16)
    fspill = nc.dram_tensor("fspill", [CH, T], F32, kind=scratch_kind)
    agy_dbg = nc.dram_tensor("agy_dbg", [2 * T, CH], BF16, kind="ExternalOutput") if debug else None
    tapd = nc.dram_tensor("tapd", [128, 4096], F32, kind="ExternalOutput") if debug else None

    stack = ExitStack()
    nc._kernel_stack = stack

    def sb(name, shape, dt):
        return stack.enter_context(nc.sbuf_tensor(name, shape, dt))

    bank = [stack.enter_context(nc.psum_tensor(f"bank{i}", [128, 512], F32))
            for i in range(8)]
    Dk, Ek, Fk, Gk, Hk = bank[3], bank[4], bank[5], bank[6], bank[7]

    ones_bf = sb("ones_bf", [128, 1], BF16)
    ident_sb = sb("ident_sb", [128, 128], F32)
    ones1_sb = sb("ones1_sb", [1, 128], F32)
    a48_sb = sb("a48_sb", [48, 1], F32)
    mask_sb = sb("mask_sb", [128, 2048], BF16)
    lsi_sb = sb("lsi_sb", [1, 4], F32)
    bv_sb = sb("bv_sb", [1, 1], F32)
    gat_sb = sb("gat_sb", [1, 1], F32)
    biasT_sb = sb("biasT_sb", [128, 16], F32)
    loraA_sb = sb("loraA_sb", [128, NCB * 48], BF16)
    loraB_sb = sb("loraB_sb", [48, CH], BF16)
    adT_sb = sb("adT_sb", [128, NCB * 10], BF16)
    m48_sb = sb("m48_sb", [48, 1], F32)
    scr1 = sb("scr1", [1, 64], F32)
    scr48 = sb("scr48", [48, 4], F32)
    gat_bc = sb("gat_bc", [128, 1], F32)
    ak_sb = sb("ak_sb", [128, NHEADS * 10], BF16)
    av_sb = sb("av_sb", [10, NHEADS * 128], BF16)
    avtmp = sb("avtmp", [128, 10], F32)
    aksq_sb = sb("aksq_sb", [128, 160], BF16)
    nrm_sb = sb("nrm_sb", [1, 64], F32)
    nbh_bc = sb("nbh_bc", [128, 16], F32)
    nbha_bc = sb("nbha_bc", [10, 16], F32)

    sems = {}

    class S:
        def __init__(self, name):
            self.h = stack.enter_context(nc.semaphore(name))
            self.n = 0

        def inc(self, instr, k=1):
            instr.then_inc(self.h, k)
            self.n += k
            return self.n
    sPE, sACT, sDVE, sCC = S("sPE"), S("sACT"), S("sDVE"), S("sCC")
    d0 = S("d0")
    dC2 = S("dC2")
    dX = [S(f"dX{i}") for i in range(4)]
    dSP = S("dSP")
    dWB = [S(f"dWB{i}") for i in range(2)]
    dST = [S(f"dST{i}") for i in range(4)]
    dKV = S("dKV")
    dQ = [S(f"dQ{i}") for i in range(2)]
    dY = [S(f"dY{i}") for i in range(2)]
    dXN2 = S("dXN2")
    dW4 = [S(f"dW4{i}") for i in range(2)]
    dXR = [S(f"dXR{i}") for i in range(2)]
    dFS = [S(f"dFS{i}") for i in range(2)]
    dYF = S("dYF")
    dW5 = [S(f"dW5{i}") for i in range(2)]
    dPL = [S(f"dPL{i}") for i in range(2)]
    dOUT = [S(f"dOUT{i}") for i in range(2)]
    dDBG = S("dDBG")
    ALL_SEMS = ([d0, dC2, dSP, dKV, dXN2, dYF, dDBG, sPE, sACT, sDVE]
                + dX + dWB + dST + dQ + dY + dW4 + dXR + dFS + dW5 + dPL + dOUT)

    blk = stack.enter_context(nc.Block())
    PE, ACT, DVE, SP, POOL = blk.tensor, blk.scalar, blk.vector, blk.sync, blk.gpsimd

    M = {}

    def p0_pool(g):
        for dst, src in [(ident_sb, identd), (ones1_sb, ones1d), (a48_sb, a48d),
                         (mask_sb, maskd), (ones_bf, onesbd), (nrm_sb, zerod),
                         (bv_sb, bvec), (gat_sb, gat), (biasT_sb, biasT)]:
            d0.inc(g.dma_start(dst[:], src[:, :]), 16)
        d0.inc(g.dma_start(lsi_sb[:], lsi[:, :]), 16)
        d0.inc(g.dma_start(bass.AP(loraA_sb, 0, [[NCB * 48, 128], [48, NCB], [1, 48]]),
                           bass.AP(loraAT, 0, [[48, 128], [128 * 48, NCB], [1, 48]])), 16)
        d0.inc(g.dma_start(loraB_sb[:], loraBT[:, :]), 16)
        d0.inc(g.dma_start(bass.AP(adT_sb, 0, [[NCB * 10, 128], [10, NCB], [1, 10]]),
                           bass.AP(adT, 0, [[10, 128], [128 * 10, NCB], [1, 10]])), 16)
        M["p0_dma"] = d0.n
    POOL(p0_pool)

    def p0_dve1(v):
        v.wait_ge(d0.h, M["p0_dma"])

        def c_(instr):
            sDVE.inc(instr)
            v.wait_ge(sDVE.h, sDVE.n)
        c_(v.tensor_single_scalar(scr1[0:1, 0:4], lsi_sb[0:1, 0:4], bv_sb[0:1, 0:1], ALU.is_le))
        c_(v.tensor_scalar_add(scr1[0:1, 4:5], scr1[0:1, 3:4], 3.0))
        c_(v.tensor_mul(scr1[0:1, 5:6], scr1[0:1, 2:3], scr1[0:1, 4:5]))
        c_(v.tensor_scalar_mul(scr1[0:1, 6:7], scr1[0:1, 1:2], 2.0))
        c_(v.tensor_scalar(scr1[0:1, 7:8], scr1[0:1, 1:2], -1.0, 1.0, ALU.mult, ALU.add))
        c_(v.tensor_mul(scr1[0:1, 8:9], scr1[0:1, 7:8], scr1[0:1, 0:1]))
        c_(v.tensor_add(scr1[0:1, 9:10], scr1[0:1, 6:7], scr1[0:1, 8:9]))
        c_(v.tensor_scalar(scr1[0:1, 10:11], scr1[0:1, 2:3], -1.0, 1.0, ALU.mult, ALU.add))
        c_(v.tensor_mul(scr1[0:1, 11:12], scr1[0:1, 10:11], scr1[0:1, 9:10]))
        c_(v.tensor_add(scr1[0:1, 12:13], scr1[0:1, 5:6], scr1[0:1, 11:12]))
        c_(v.tensor_scalar_add(scr1[0:1, 13:14], scr1[0:1, 12:13], -1.0))
        c_(v.tensor_scalar(scr1[0:1, 14:15], scr1[0:1, 13:14], -1.0, 0.0, ALU.add, ALU.max))
        c_(v.tensor_scalar_min(scr1[0:1, 15:16], scr1[0:1, 14:15], 2.0))
        c_(v.tensor_single_scalar(scr1[0:1, 16:17], scr1[0:1, 13:14], 0.5, ALU.is_ge))
        M["p0_dve1"] = sDVE.n
    DVE(p0_dve1)

    def p0_pe1(t):
        t.wait_ge(sDVE.h, M["p0_dve1"])
        t.matmul(Ek[0:48, 0:1], ones1_sb[0:1, 0:48], scr1[0:1, 15:16], start=True, stop=True)
        t.matmul(Ek[0:48, 1:2], ones1_sb[0:1, 0:48], scr1[0:1, 16:17], start=True, stop=True)
        sPE.inc(t.matmul(Fk[0:128, 0:1], ones1_sb[0:1, 0:128], gat_sb[0:1, 0:1],
                         start=True, stop=True))
        M["p0_pe1"] = sPE.n
    PE(p0_pe1)

    def p0_dve2(v):
        v.wait_ge(sPE.h, M["p0_pe1"])

        def c_(instr):
            sDVE.inc(instr)
            v.wait_ge(sDVE.h, sDVE.n)
        c_(v.tensor_copy(gat_bc[:], Fk[0:128, 0:1]))
        c_(v.tensor_sub(scr48[0:48, 0:1], a48_sb[0:48, 0:1], Ek[0:48, 0:1]))
        c_(v.tensor_scalar_mul(scr48[0:48, 1:2], scr48[0:48, 0:1], -1.0))
        c_(v.tensor_max(scr48[0:48, 1:2], scr48[0:48, 0:1], scr48[0:48, 1:2]))
        c_(v.tensor_scalar(scr48[0:48, 2:3], scr48[0:48, 1:2], -1.0, 1.0, ALU.mult, ALU.add))
        c_(v.tensor_scalar_max(scr48[0:48, 3:4], scr48[0:48, 2:3], 0.0))
        c_(v.tensor_mul(m48_sb[0:48, 0:1], scr48[0:48, 3:4], Ek[0:48, 1:2]))
        M["p0_mask"] = sDVE.n
    DVE(p0_dve2)

    g12 = ExitStack()
    xn = g12.enter_context(nc.sbuf_tensor("xn", [128, NCB * T], BF16))
    g1 = ExitStack()
    m_bc = g1.enter_context(nc.sbuf_tensor("m_bc", [128, T], F32))
    r_bc = g1.enter_context(nc.sbuf_tensor("r_bc", [128, T], F32))
    sqb = [g1.enter_context(nc.sbuf_tensor(f"sqb{i}", [128, T], BF16)) for i in range(2)]
    mrow = g1.enter_context(nc.sbuf_tensor("mrow", [1, T], F32))
    vrow = g1.enter_context(nc.sbuf_tensor("vrow", [1, T], F32))
    rrow = g1.enter_context(nc.sbuf_tensor("rrow", [1, T], F32))

    xld = []

    def p1_load(g):
        g.wait_ge(d0.h, M["p0_dma"])
        for cb in range(NCB):
            s_ = dX[cb % 4]
            if cb >= 4:
                g.wait_ge(s_.h, s_.n)
            s_.inc(g.dma_start(xn[:, cb * T:(cb + 1) * T], xT[cb * 128:(cb + 1) * 128, :]), 16)
            xld.append((s_.h, s_.n))
    POOL(p1_load)

    sq_done, sq_cons = {}, {}
    for cb in range(NCB):
        def p1_sq(s, cb=cb):
            s.wait_ge(*xld[cb])
            if cb >= 2:
                s.wait_ge(sPE.h, sq_cons[cb - 2])
            sACT.inc(s.activation(sqb[cb % 2][:, :], xn[:, cb * T:(cb + 1) * T], AF.Square))
            sq_done[cb] = sACT.n
        ACT(p1_sq)

        def p1_mm(t, cb=cb):
            st, sp = (cb == 0), (cb == NCB - 1)
            t.wait_ge(*xld[cb])
            if cb == 0:
                t.wait_ge(sDVE.h, M["p0_mask"])
            for tsl in range(NTS):
                t.matmul(bank[tsl][0:1, :], ones_bf[:, 0:1],
                         xn[:, cb * T + tsl * 512:cb * T + (tsl + 1) * 512],
                         start=st, stop=sp)
            t.wait_ge(sACT.h, sq_done[cb])
            ins = None
            for tsl in range(NTS):
                ins = t.matmul(bank[4 + tsl][0:1, :], ones_bf[:, 0:1],
                               sqb[cb % 2][:, tsl * 512:(tsl + 1) * 512],
                               start=st, stop=sp)
            sPE.inc(ins)
            sq_cons[cb] = sPE.n
        PE(p1_mm)
    M["p1_stats"] = sPE.n

    def p1_stats(v):
        v.wait_ge(sPE.h, M["p1_stats"])

        def c_(instr):
            sDVE.inc(instr)
            v.wait_ge(sDVE.h, sDVE.n)
        for tsl in range(NTS):
            sl = slice(tsl * 512, (tsl + 1) * 512)
            c_(v.tensor_scalar_mul(mrow[0:1, sl], bank[tsl][0:1, :], 1.0 / C))
            c_(v.tensor_mul(vrow[0:1, sl], mrow[0:1, sl], mrow[0:1, sl]))
            c_(v.scalar_tensor_tensor(vrow[0:1, sl], bank[4 + tsl][0:1, :], 1.0 / C,
                                      vrow[0:1, sl], ALU.mult, ALU.subtract))
        M["p1_var"] = sDVE.n
    DVE(p1_stats)

    def p1_rstd(s):
        s.wait_ge(sDVE.h, M["p1_var"])
        sACT.inc(s.activation(rrow[0:1, :], vrow[0:1, :], AF.Ln, bias=np.float32(EPS), scale=np.float32(1.0)))
        s.wait_ge(sACT.h, sACT.n)
        sACT.inc(s.activation(rrow[0:1, :], rrow[0:1, :], AF.Exp, scale=np.float32(-0.5)))
        M["p1_rstd"] = sACT.n
    ACT(p1_rstd)

    def p1_bcast(t):
        t.wait_ge(sACT.h, M["p1_rstd"])
        ins = None
        for tsl in range(NTS):
            sl = slice(tsl * 512, (tsl + 1) * 512)
            t.matmul(bank[tsl][:, :], ones1_sb[0:1, 0:128], mrow[0:1, sl], start=True, stop=True)
            ins = t.matmul(bank[4 + tsl][:, :], ones1_sb[0:1, 0:128], rrow[0:1, sl],
                           start=True, stop=True)
        sPE.inc(ins)
        M["p1_bc"] = sPE.n
    PE(p1_bcast)

    def p1_bcsb(v):
        v.wait_ge(sPE.h, M["p1_bc"])
        for tsl in range(NTS):
            sl = slice(tsl * 512, (tsl + 1) * 512)
            sDVE.inc(v.tensor_copy(m_bc[:, sl], bank[tsl][:, :]))
            sDVE.inc(v.tensor_copy(r_bc[:, sl], bank[4 + tsl][:, :]))
        M["p1_bcsb"] = sDVE.n
    DVE(p1_bcsb)

    xn_ready = {}
    for cb in range(NCB):
        def p1_norm(v, cb=cb):
            sl = slice(cb * T, (cb + 1) * T)
            if cb == 0:
                v.wait_ge(sDVE.h, M["p1_bcsb"])
            else:
                v.wait_ge(sDVE.h, xn_ready[cb - 1])
            sDVE.inc(v.tensor_sub(xn[:, sl], xn[:, sl], m_bc[:, :]))
            v.wait_ge(sDVE.h, sDVE.n)
            sDVE.inc(v.tensor_mul(xn[:, sl], xn[:, sl], r_bc[:, :]))
            xn_ready[cb] = sDVE.n
        DVE(p1_norm)

        def p1_spill(sy, cb=cb):
            sy.wait_ge(sDVE.h, xn_ready[cb])
            dSP.inc(sy.dma_start(xnscr[cb * 128:(cb + 1) * 128, :], xn[:, cb * T:(cb + 1) * T]), 16)
        SP(p1_spill)
    M["p1_xn"] = sDVE.n
    M["p1_spill"] = dSP.n

    def _fin(sy):
        for s_ in ALL_SEMS:
            if s_.n:
                sy.wait_ge(s_.h, s_.n)
    if upto <= 1:
        SP(_fin)
        return nc

    g1.close()
    g2 = ExitStack()
    cos_sb = g2.enter_context(nc.sbuf_tensor("cos_sb", [128, T], F32))
    sin_sb = g2.enter_context(nc.sbuf_tensor("sin_sb", [128, T], F32))
    wband = [g2.enter_context(nc.sbuf_tensor(f"wband{i}", [128, NCB * 128], BF16))
             for i in range(2)]
    stg = [g2.enter_context(nc.sbuf_tensor(f"stg{i}", [128, 512], BF16)) for i in range(4)]
    sqst = [g2.enter_context(nc.sbuf_tensor(f"sqst{i}", [128, 512], BF16)) for i in range(4)]
    t1_sb = g2.enter_context(nc.sbuf_tensor("t1_sb", [128, 512], F32))
    t2_sb = g2.enter_context(nc.sbuf_tensor("t2_sb", [128, 512], F32))

    def p2_consts(g):
        g.wait_ge(sDVE.h, M["p1_xn"])
        dC2.inc(g.dma_start(cos_sb[:], cosd[:, :]), 16)
        dC2.inc(g.dma_start(sin_sb[:], sind[:, :]), 16)
        M["p2_consts"] = dC2.n
    POOL(p2_consts)

    wld, wcons = {}, {}
    grp = {}
    evac = {}
    store = {}
    sqm = {}
    nm_done = {}
    nrm_read = {}
    adcopy = {}
    avtr = {}
    avcp = {}

    def p2_wload(g, og):
        if og >= 2:
            g.wait_ge(sPE.h, grp[(og - 2, NTS - 1)])
        s_ = dWB[og % 2]
        s_.inc(g.dma_start(
            bass.AP(wband[og % 2], 0, [[NCB * 128, 128], [128, NCB], [1, 128]]),
            bass.AP(wqkvT, og * 128, [[QKVO, 128], [128 * QKVO, NCB], [1, 128]])), 16)
        wld[og] = (s_.h, s_.n)

    POOL(lambda g: (p2_wload(g, 0), p2_wload(g, 1)))

    last_nrm_idx = [None]

    def emit_norm_lagged(idx):
        og, tsl = divmod(idx, NTS)
        if og >= 32:
            return

        def pe_nm(t):
            t.wait_ge(sACT.h, sqm[idx])
            if last_nrm_idx[0] is not None:
                t.wait_ge(sDVE.h, nrm_read[last_nrm_idx[0]])
            sPE.inc(t.matmul(Ek[0:1, 0:512], ones_bf[:, 0:1], sqst[idx % 4][:, :],
                             start=True, stop=True))
            nm_done[idx] = sPE.n
        PE(pe_nm)

        def dve_nr(v):
            v.wait_ge(sPE.h, nm_done[idx])
            sDVE.inc(v.reduce_max(scr1[0:1, 20:21], Ek[0:1, 0:512], axis=AX.X))
            v.wait_ge(sDVE.h, sDVE.n)
            sDVE.inc(v.tensor_max(nrm_sb[0:1, og:og + 1], nrm_sb[0:1, og:og + 1],
                                  scr1[0:1, 20:21]))
            nrm_read[idx] = sDVE.n
        DVE(dve_nr)
        last_nrm_idx[0] = idx

    for og in range(NOG):
        for tsl in range(NTS):
            idx = og * NTS + tsl

            def p2_grp(t, og=og, tsl=tsl, idx=idx):
                if tsl == 0:
                    t.wait_ge(*wld[og])
                    if og == 0:
                        t.wait_ge(sDVE.h, M["p1_xn"])
                    if og > 16 and (og - 1) >= 16:
                        t.wait_ge(sDVE.h, adcopy[og - 1])
                if idx >= 3:
                    t.wait_ge(*evac[idx - 3])
                rb = bank[idx % 3]
                ins = None
                for cb in range(NCB):
                    ins = t.matmul(rb[:, :], wband[og % 2][:, cb * 128:(cb + 1) * 128],
                                   xn[:, cb * T + tsl * 512:cb * T + (tsl + 1) * 512],
                                   start=(cb == 0), stop=(cb == NCB - 1))
                    if tsl == 0 and og >= 16:
                        ins = t.matmul(Dk[:, 0:10], wband[og % 2][:, cb * 128:(cb + 1) * 128],
                                       adT_sb[:, cb * 10:(cb + 1) * 10],
                                       start=(cb == 0), stop=(cb == NCB - 1))
                sPE.inc(ins)
                grp[(og, tsl)] = sPE.n
            PE(p2_grp)

            if og < 32:
                def p2_rope(v, og=og, tsl=tsl, idx=idx):
                    v.wait_ge(sPE.h, grp[(og, tsl)])
                    if idx == 0:
                        v.wait_ge(dC2.h, M["p2_consts"])
                    if idx >= 4:
                        v.wait_ge(*store[idx - 4])
                        if (idx - 4) // NTS < 32:
                            v.wait_ge(sACT.h, sqm[idx - 4])
                    if idx >= 1 and (idx - 1) in evac and evac[idx - 1][0] is sDVE.h:
                        v.wait_ge(*evac[idx - 1])
                    rb = bank[idx % 3]
                    sl = slice(tsl * 512, (tsl + 1) * 512)
                    sDVE.inc(v.tensor_mul(t1_sb[:, :], rb[:, :], cos_sb[:, sl]))
                    sDVE.inc(v.tensor_mul(t2_sb[0:64, :], rb[64:128, :], sin_sb[0:64, sl]))
                    sDVE.inc(v.tensor_mul(t2_sb[64:128, :], rb[0:64, :], sin_sb[64:128, sl]))
                    v.wait_ge(sDVE.h, sDVE.n)
                    sDVE.inc(v.tensor_add(stg[idx % 4][:, :], t1_sb[:, :], t2_sb[:, :]))
                    evac[idx] = (sDVE.h, sDVE.n)
                DVE(p2_rope)

                pass
            else:
                def p2_vcp(v, og=og, tsl=tsl, idx=idx):
                    v.wait_ge(sPE.h, grp[(og, tsl)])
                    if idx >= 4:
                        v.wait_ge(*store[idx - 4])
                    sDVE.inc(v.tensor_copy(stg[idx % 4][:, :], bank[idx % 3][:, :]))
                    evac[idx] = (sDVE.h, sDVE.n)
                DVE(p2_vcp)

            def p2_store(sy, og=og, tsl=tsl, idx=idx):
                sy.wait_ge(*evac[idx])
                tgt = qscr if og < 16 else (kscr if og < 32 else vscr)
                row = (og % 16) * 128
                s_ = dST[idx % 4]
                s_.inc(sy.dma_start(tgt[row:row + 128, tsl * 512:(tsl + 1) * 512],
                                    stg[idx % 4][:, :]), 16)
                store[idx] = (s_.h, s_.n)
            SP(p2_store)



            if tsl == 0 and og >= 16:
                def p2_adcp(v, og=og):
                    v.wait_ge(sPE.h, grp[(og, 0)])
                    if og >= 32 and og - 1 >= 32:
                        v.wait_ge(sPE.h, avtr[og - 1])
                    if og < 32:
                        ins = v.tensor_copy(ak_sb[:, (og - 16) * 10:(og - 15) * 10], Dk[:, 0:10])
                    else:
                        ins = v.tensor_copy(avtmp[:, :], Dk[:, 0:10])
                    sDVE.inc(ins)
                    adcopy[og] = sDVE.n
                DVE(p2_adcp)

            if tsl == 1 and og >= 32:
                def p2_avtr(t, og=og):
                    t.wait_ge(sDVE.h, adcopy[og])
                    if og - 1 >= 32:
                        t.wait_ge(sDVE.h, avcp[og - 1])
                    sPE.inc(t.transpose(Fk[0:10, 0:128], avtmp[:, :], ident_sb[:, :]))
                    avtr[og] = sPE.n
                PE(p2_avtr)

                def p2_avcp(v, og=og):
                    v.wait_ge(sPE.h, avtr[og])
                    sDVE.inc(v.tensor_copy(av_sb[:, (og - 32) * 128:(og - 31) * 128],
                                           Fk[0:10, 0:128]))
                    avcp[og] = sDVE.n
                DVE(p2_avcp)

        if og >= 1 and og - 1 < 32:
            for t2 in range(NTS):
                emit_norm_lagged((og - 1) * NTS + t2)
        if og < 32:
            def p2_sq4(s, og=og):
                for tsl in range(NTS):
                    idx = og * NTS + tsl
                    s.wait_ge(*evac[idx])
                    if idx >= 4 and (idx - 4) // NTS < 32:
                        s.wait_ge(sPE.h, nm_done[idx - 4])
                    sACT.inc(s.activation(sqst[idx % 4][:, :], stg[idx % 4][:, :], AF.Square))
                    sqm[idx] = sACT.n
            ACT(p2_sq4)
        if og + 2 < NOG:
            POOL(lambda g, og=og: p2_wload(g, og + 2))

    for t2 in range(NTS):
        emit_norm_lagged(31 * NTS + t2)
    M["p2_pe"] = sPE.n
    M["p2_store"] = [(s_.h, s_.n) for s_ in dST]

    def p2_aksq(s):
        s.wait_ge(sDVE.h, adcopy[31])
        sACT.inc(s.activation(aksq_sb[:, :], ak_sb[:, :], AF.Square))
        M["p2_aksq"] = sACT.n
    ACT(p2_aksq)

    def p2_aknm(t):
        t.wait_ge(sACT.h, M["p2_aksq"])
        t.wait_ge(sDVE.h, nrm_read[last_nrm_idx[0]])
        sPE.inc(t.matmul(Ek[0:1, 0:160], ones_bf[:, 0:1], aksq_sb[:, :], start=True, stop=True))
        M["p2_aknm"] = sPE.n
    PE(p2_aknm)

    def p2_akred(v):
        v.wait_ge(sPE.h, M["p2_aknm"])
        for h in range(NHEADS):
            sDVE.inc(v.reduce_max(nrm_sb[0:1, 32 + h:33 + h], Ek[0:1, h * 10:(h + 1) * 10],
                                  axis=AX.X))
        M["p2_akred"] = sDVE.n
    DVE(p2_akred)

    def p2_ln(s):
        s.wait_ge(sDVE.h, M["p2_akred"])
        sACT.inc(s.activation(scr1[0:1, 32:48], nrm_sb[0:1, 32:48], AF.Ln, bias=np.float32(1e-30)))
        sACT.inc(s.activation(scr1[0:1, 0:32], nrm_sb[0:1, 0:32], AF.Ln, bias=np.float32(1e-30)))
        M["p2_ln"] = sACT.n
    ACT(p2_ln)

    def p2_lns(v):
        v.wait_ge(sACT.h, M["p2_ln"])
        sDVE.inc(v.tensor_add(scr1[0:1, 48:64], scr1[0:1, 0:16], scr1[0:1, 16:32]))
        sDVE.inc(v.tensor_add(scr1[0:1, 16:32], scr1[0:1, 0:16], scr1[0:1, 32:48]))
        M["p2_lns"] = sDVE.n
    DVE(p2_lns)

    def p2_exp(s):
        s.wait_ge(sDVE.h, M["p2_lns"])
        sACT.inc(s.activation(scr1[0:1, 48:64], scr1[0:1, 48:64], AF.Exp, scale=np.float32(0.5)))
        sACT.inc(s.activation(scr1[0:1, 16:32], scr1[0:1, 16:32], AF.Exp, scale=np.float32(0.5)))
        M["p2_exp"] = sACT.n
    ACT(p2_exp)

    def p2_nbh(v):
        v.wait_ge(sACT.h, M["p2_exp"])
        sDVE.inc(v.tensor_scalar_mul(scr1[0:1, 48:64], scr1[0:1, 48:64], -S_SCALE * 1.0002))
        sDVE.inc(v.tensor_scalar_mul(scr1[0:1, 16:32], scr1[0:1, 16:32], -A_SCALE * 1.0002))
        M["p2_nbh"] = sDVE.n
    DVE(p2_nbh)

    def p2_nbhbc(t):
        t.wait_ge(sDVE.h, M["p2_nbh"])
        t.matmul(Gk[:, 0:16], ones1_sb[0:1, 0:128], scr1[0:1, 48:64], start=True, stop=True)
        sPE.inc(t.matmul(Gk[0:10, 16:32], ones1_sb[0:1, 0:10], scr1[0:1, 16:32],
                         start=True, stop=True))
        M["p2_nbhbc"] = sPE.n
    PE(p2_nbhbc)

    def p2_nbhsb(v):
        v.wait_ge(sPE.h, M["p2_nbhbc"])
        sDVE.inc(v.tensor_copy(nbh_bc[:, :], Gk[:, 0:16]))
        sDVE.inc(v.tensor_copy(nbha_bc[:, :], Gk[0:10, 16:32]))
        M["p2_nbhsb"] = sDVE.n
    DVE(p2_nbhsb)

    M["p2_act"] = sACT.n
    M["p2_dve"] = sDVE.n

    def _fin(sy):
        for s_ in ALL_SEMS:
            if s_.n:
                sy.wait_ge(s_.h, s_.n)
    if upto <= 2:
        SP(_fin)
        return nc
    g2.close()
    g12.close()

    g3 = ExitStack()
    k_fm = g3.enter_context(nc.sbuf_tensor("k_fm", [128, NHEADS * T], BF16))
    v_tm = g3.enter_context(nc.sbuf_tensor("v_tm", [128, 16 * CH], BF16))
    at_sb = g3.enter_context(nc.sbuf_tensor("at_sb", [128, 2 * 16 * 512], BF16))
    ystg = [g3.enter_context(nc.sbuf_tensor(f"ystg{i}", [128, 512], BF16)) for i in range(2)]
    q_sb = [g3.enter_context(nc.sbuf_tensor(f"q_sb{i}", [128, 512], BF16)) for i in range(2)]
    expaT = g3.enter_context(nc.sbuf_tensor("expaT", [10, 512], BF16))
    expaS = g3.enter_context(nc.sbuf_tensor("expaS", [10, 512], BF16))
    abcs = g3.enter_context(nc.sbuf_tensor("abcs", [10, 512], F32))
    ra_row = g3.enter_context(nc.sbuf_tensor("ra_row", [1, 512], F32))
    recm = g3.enter_context(nc.sbuf_tensor("recm", [128, 8], F32))

    def atsl(slot, c):
        return at_sb[:, (slot * 16 + c) * 512:(slot * 16 + c) * 512 + 512]

    kvld = {}
    qld = {}

    def p3_kv(sy):
        for mk in M["p2_store"]:
            sy.wait_ge(*mk)
        sy.wait_ge(sPE.h, M["p2_pe"])
        sy.wait_ge(sACT.h, M["p2_act"])
        sy.wait_ge(sDVE.h, M["p2_dve"])
        for h in range(NHEADS):
            dKV.inc(sy.dma_start(k_fm[:, h * T:(h + 1) * T], kscr[h * 128:(h + 1) * 128, :]), 16)
        for tc in range(16):
            dKV.inc(sy.dma_start(v_tm[:, tc * CH:(tc + 1) * CH],
                                 vscr[0:CH, tc * 128:(tc + 1) * 128], transpose=True), 16)
        M["p3_kv"] = dKV.n
        for gen in (0, 1):
            h, sl = divmod(gen, NTS)
            s_ = dQ[gen % 2]
            s_.inc(sy.dma_start(q_sb[gen % 2][:, :],
                                qscr[h * 128:(h + 1) * 128, sl * 512:(sl + 1) * 512]), 16)
            qld[gen] = (s_.h, s_.n)
    SP(p3_kv)

    expm = {}
    maskm = {}
    scorem = {}
    bank_free = [(sACT.h, 0), (sACT.h, 0)]
    sums_done = {}
    y_done = {}
    ascm = {}
    aexpm = {}
    asumm = {}
    arrm = {}
    abcm = {}
    adwm = {}
    recmm = {}
    evacm = {}
    yaddm = {}
    ystorem = {}
    sums_free = {0: (sDVE.h, 0), 1: (sDVE.h, 0)}

    SCB = [bank[0], bank[1], bank[5]]
    exp_gi = []
    gi_base = {}
    NGEN = NHEADS * NTS

    def emit_tail(gen):
        h, sl = divmod(gen, NTS)
        nch = 4 * sl + 4

        _dbg_act, _dbg_dve = sACT.n, sDVE.n
        def p3_peT1(t, h=h, sl=sl, gen=gen, nch=nch, da=_dbg_act, dv=_dbg_dve):
            import os
            if os.environ.get("P3SERIAL"):
                t.wait_ge(sACT.h, da)
                t.wait_ge(sDVE.h, dv)
            t.wait_ge(sACT.h, expm[(gen, nch - 1)])
            t.wait_ge(sDVE.h, maskm[(gen, nch - 1)])
            if gen >= 1:
                t.wait_ge(sDVE.h, recmm[gen - 1])
                for j in range(4):
                    t.wait_ge(*evacm[(gen - 1, j)])
            for j in range(4):
                for c in range(nch):
                    lhs = at_sb[:, ((gen % 2) * 16 + c) * 512 + j * 128:
                                ((gen % 2) * 16 + c) * 512 + (j + 1) * 128]
                    t.matmul(bank[2 + j // 2][:, (j % 2) * 128:(j % 2) * 128 + 128], lhs,
                             v_tm[:, c * CH + h * 128:c * CH + (h + 1) * 128],
                             start=(c == 0), stop=False)
                    ins = t.matmul(bank[4][:, j:j + 1], lhs, ones_bf[:, 0:1],
                                   start=(c == 0), stop=(c == nch - 1))
                sPE.inc(ins)
                sums_done[(gen, j)] = sPE.n
            t.wait_ge(sACT.h, aexpm[gen])
            sPE.inc(t.matmul(Gk[0:1, 0:512], ones_bf[0:10, 0:1], expaT[0:10, :],
                             start=True, stop=True))
            asumm[gen] = sPE.n
        PE(p3_peT1)

        if debug and gen == 0:
            def p3_tapst(g_):
                g_.wait_ge(sPE.h, sums_done[(0, 3)])
                dDBG.inc(g_.dma_start(tapd[0:128, 0:2048], at_sb[:, 0:2048]), 16)
                M["tap2"] = dDBG.n
            POOL(p3_tapst)

        def p3_arr(v, gen=gen):
            v.wait_ge(sPE.h, asumm[gen])
            sDVE.inc(v.reciprocal(ra_row[0:1, :], Gk[0:1, 0:512]))
            v.wait_ge(sDVE.h, sDVE.n)
            sDVE.inc(v.tensor_single_scalar(ra_row[0:1, :], ra_row[0:1, :],
                                            gat_sb[0:1, 0:1], ALU.mult))
            arrm[gen] = sDVE.n
        DVE(p3_arr)

        def p3_abc(t, gen=gen):
            t.wait_ge(sDVE.h, arrm[gen])
            sPE.inc(t.matmul(Gk[0:10, 0:512], ones1_sb[0:1, 0:10], ra_row[0:1, :],
                             start=True, stop=True))
            abcm[gen] = sPE.n
        PE(p3_abc)

        def p3_adw(v, gen=gen):
            v.wait_ge(sPE.h, abcm[gen])
            if gen >= 1:
                v.wait_ge(sPE.h, y_done[gen - 1])
            sDVE.inc(v.tensor_copy(abcs[0:10, :], Gk[0:10, 0:512]))
            v.wait_ge(sDVE.h, sDVE.n)
            sDVE.inc(v.tensor_mul(expaS[0:10, :], expaT[0:10, :], abcs[0:10, :]))
            adwm[gen] = sDVE.n
        DVE(p3_adw)

        def p3_peT3(t, h=h, gen=gen):
            t.wait_ge(sDVE.h, adwm[gen])
            if gen >= 1:
                t.wait_ge(sDVE.h, yaddm[gen - 1])
            ins = None
            for j in range(4):
                sPE.inc(t.matmul(Hk[:, j * 128:(j + 1) * 128],
                               expaS[0:10, j * 128:(j + 1) * 128],
                               av_sb[0:10, h * 128:(h + 1) * 128],
                               start=True, stop=True))
            y_done[gen] = sPE.n
            if gen + 1 < NGEN:
                g2 = gen + 1
                h2 = g2 // NTS
                sPE.inc(t.matmul(Gk[0:10, 0:512], ak_sb[:, h2 * 10:(h2 + 1) * 10],
                                 q_sb[g2 % 2][:, :], start=True, stop=True))
                ascm[g2] = sPE.n
        PE(p3_peT3)

        def p3_epi(v, gen=gen):
            v.wait_ge(sPE.h, sums_done[(gen, 3)])
            if gen >= 2:
                v.wait_ge(*evacm[(gen - 2, 3)])
                v.wait_ge(*ystorem[gen - 2])
            for j in range(4):
                sDVE.inc(v.reciprocal(recm[:, (gen % 2) * 4 + j:(gen % 2) * 4 + j + 1],
                                      bank[4][:, j:j + 1]))
            recmm[gen] = sDVE.n
            v.wait_ge(sDVE.h, sDVE.n)
            for j in range(4):
                sDVE.inc(v.tensor_single_scalar(
                    ystg[gen % 2][:, j * 128:(j + 1) * 128],
                    bank[2 + j // 2][:, (j % 2) * 128:(j % 2) * 128 + 128],
                    recm[:, (gen % 2) * 4 + j:(gen % 2) * 4 + j + 1], ALU.mult))
                evacm[(gen, j)] = (sDVE.h, sDVE.n)
            v.wait_ge(sPE.h, y_done[gen])
            v.wait_ge(sDVE.h, sDVE.n)
            sDVE.inc(v.tensor_add(ystg[gen % 2][:, :], ystg[gen % 2][:, :], Hk[:, 0:512]))
            yaddm[gen] = sDVE.n
        DVE(p3_epi)

        def p3_yst(sy, h=h, sl=sl, gen=gen):
            sy.wait_ge(sDVE.h, yaddm[gen])
            s_ = dY[gen % 2]
            s_.inc(sy.dma_start(
                bass.AP(yscr, sl * 512 * CH + h * 128, [[CH, 128], [128 * CH, 4], [1, 128]]),
                bass.AP(ystg[gen % 2], 0, [[512, 128], [128, 4], [1, 128]])), 16)
            ystorem[gen] = (s_.h, s_.n)
        SP(p3_yst)

    for h in range(NHEADS):
        for sl in range(NTS):
            gen = h * NTS + sl
            nch = 4 * sl + 4
            gi_base[gen] = len(exp_gi)
            act_base = sACT.n
            has_aexp_prev = 1 if gen >= 1 else 0

            def p3_peA(t, h=h, gen=gen, nch=nch, ab=act_base, hp=has_aexp_prev):
                t.wait_ge(*qld[gen])
                if gen == 0:
                    t.wait_ge(dKV.h, M["p3_kv"])
                    t.wait_ge(sDVE.h, M["p2_nbhsb"])
                for c in range(nch):
                    gi = gi_base[gen] + c
                    if gi >= 3:
                        t.wait_ge(sACT.h, exp_gi[gi - 3])
                    sPE.inc(t.matmul(SCB[gi % 3][:, :],
                                     k_fm[:, h * T + c * 128:h * T + (c + 1) * 128],
                                     q_sb[gen % 2][:, :], start=True, stop=True))
                    scorem[(gen, c)] = sPE.n
                    exp_gi.append(ab + hp + c + 1)
                if gen == 0:
                    sPE.inc(t.matmul(Gk[0:10, 0:512], ak_sb[:, 0:10],
                                     q_sb[0][:, :], start=True, stop=True))
                    ascm[0] = sPE.n
            PE(p3_peA)

            def p3_act(s, h=h, gen=gen, nch=nch, ab=act_base, hp=has_aexp_prev):
                if hp:
                    s.wait_ge(sPE.h, ascm[gen - 1])
                    if gen >= 2:
                        s.wait_ge(sDVE.h, adwm[gen - 2])
                    hprev = (gen - 1) // NTS
                    sACT.inc(s.activation(expaT[0:10, :], Gk[0:10, 0:512], AF.Exp,
                                          bias=nbha_bc[:, hprev:hprev + 1],
                                          scale=np.float32(A_SCALE)))
                    aexpm[gen - 1] = sACT.n
                for c in range(nch):
                    s.wait_ge(sPE.h, scorem[(gen, c)])
                    if c == 0 and gen >= 2:
                        s.wait_ge(sPE.h, y_done[gen - 2])
                    sACT.inc(s.activation(atsl(gen % 2, c), SCB[(gi_base[gen] + c) % 3][:, :],
                                          AF.Exp, bias=nbh_bc[:, h:h + 1],
                                          scale=np.float32(S_SCALE)))
                    assert sACT.n == ab + hp + c + 1
                    expm[(gen, c)] = sACT.n
            ACT(p3_act)

            def p3_masks(v, gen=gen, sl=sl, nch=nch):
                for c in range(4 * sl, nch):
                    v.wait_ge(sACT.h, expm[(gen, c)])
                    r = c - 4 * sl
                    sDVE.inc(v.tensor_mul(atsl(gen % 2, c), atsl(gen % 2, c),
                                          mask_sb[:, r * 512:(r + 1) * 512]))
                    maskm[(gen, c)] = sDVE.n
            DVE(p3_masks)

            if gen >= 1:
                emit_tail(gen - 1)

            if gen + 2 < NGEN:
                def p3_q(sy, gen=gen):
                    g2 = gen + 2
                    h2, sl2 = divmod(g2, NTS)
                    sy.wait_ge(sPE.h, ascm[gen])
                    s_ = dQ[g2 % 2]
                    s_.inc(sy.dma_start(q_sb[g2 % 2][:, :],
                                        qscr[h2 * 128:(h2 + 1) * 128,
                                             sl2 * 512:(sl2 + 1) * 512]), 16)
                    qld[g2] = (s_.h, s_.n)
                SP(p3_q)

    def p3_aexp_last(s):
        s.wait_ge(sPE.h, ascm[NGEN - 1])
        s.wait_ge(sDVE.h, adwm[NGEN - 2])
        hl = (NGEN - 1) // NTS
        sACT.inc(s.activation(expaT[0:10, :], Gk[0:10, 0:512], AF.Exp,
                              bias=nbha_bc[:, hl:hl + 1], scale=np.float32(A_SCALE)))
        aexpm[NGEN - 1] = sACT.n
    ACT(p3_aexp_last)
    emit_tail(NGEN - 1)

    LASTG = NHEADS * NTS - 1

    def p3_ag(g):
        g.wait_ge(dY[0].h, dY[0].n)
        g.wait_ge(dY[1].h, dY[1].n)
        sCC.inc(g.collective_compute(
            "AllGather", ALU.bypass,
            replica_groups=[[0, 1], [2, 3], [4, 5], [6, 7]],
            ins=[yscr.ap()], outs=[agy.ap()]))
        M["ag"] = sCC.n
        if debug:
            g.wait_ge(sCC.h, M["ag"])
            dDBG.inc(g.dma_start(agy_dbg[:, :], agy[:, :]), 16)
            g.wait_ge(dDBG.h, dDBG.n)
    POOL(p3_ag)

    M["p3_pe"] = sPE.n
    M["p3_act"] = sACT.n
    M["p3_dve"] = sDVE.n

    def _fin(sy):
        for s_ in ALL_SEMS:
            if s_.n:
                sy.wait_ge(s_.h, s_.n)
    if upto <= 3:
        SP(_fin)
        return nc
    g3.close()

    g4 = ExitStack()
    xn2 = g4.enter_context(nc.sbuf_tensor("xn2", [128, NCB * T], BF16))
    wband4 = [g4.enter_context(nc.sbuf_tensor(f"wband4_{i}", [128, NCB * 128], BF16))
              for i in range(2)]
    mrT_sb = g4.enter_context(nc.sbuf_tensor("mrT_sb", [48, T], BF16))
    xstg = [g4.enter_context(nc.sbuf_tensor(f"xstg{i}", [128, 512], F32)) for i in range(2)]
    sstg = [g4.enter_context(nc.sbuf_tensor(f"sstg{i}", [128, 512], F32)) for i in range(2)]

    xn2ld = {}

    def p4_xn(sy):
        sy.wait_ge(sPE.h, M["p3_pe"])
        sy.wait_ge(sACT.h, M["p3_act"])
        sy.wait_ge(sDVE.h, M["p3_dve"])
        sy.wait_ge(dSP.h, M["p1_spill"])
        for cb in range(NCB):
            dXN2.inc(sy.dma_start(xn2[:, cb * T:(cb + 1) * T], xnscr[cb * 128:(cb + 1) * 128, :]), 16)
        M["p4_xn"] = dXN2.n
    SP(p4_xn)

    w4ld, w4grp = {}, {}

    def p4_wload(g, og):
        if og >= 2:
            g.wait_ge(sPE.h, w4grp[(og - 2, NTS - 1)])
        s_ = dW4[og % 2]
        s_.inc(g.dma_start(
            bass.AP(wband4[og % 2], 0, [[NCB * 128, 128], [128, NCB], [1, 128]]),
            bass.AP(wffnT, og * 128, [[CH, 128], [128 * CH, NCB], [1, 128]])), 16)
        w4ld[og] = (s_.h, s_.n)

    POOL(lambda g: (g.wait_ge(sPE.h, M["p3_pe"]), p4_wload(g, 0), p4_wload(g, 1)))

    mrm = {}
    for tsl in range(NTS):
        def p4_mr(t, tsl=tsl):
            if tsl == 0:
                t.wait_ge(dXN2.h, M["p4_xn"])
            else:
                t.wait_ge(sDVE.h, mrm[("s", tsl - 1)])
            ins = None
            for cb in range(NCB):
                sPE.inc(t.matmul(Dk[0:48, 0:512], loraA_sb[:, cb * 48:(cb + 1) * 48],
                               xn2[:, cb * T + tsl * 512:cb * T + (tsl + 1) * 512],
                               start=(cb == 0), stop=(cb == NCB - 1)))
            mrm[("g", tsl)] = sPE.n
        PE(p4_mr)

        def p4_mrs(v, tsl=tsl):
            v.wait_ge(sPE.h, mrm[("g", tsl)])
            sDVE.inc(v.tensor_single_scalar(mrT_sb[0:48, tsl * 512:(tsl + 1) * 512],
                                            Dk[0:48, 0:512], m48_sb[0:48, 0:1], ALU.mult))
            mrm[("s", tsl)] = sDVE.n
        DVE(p4_mrs)

    fev = {}
    fst = {}
    xres = {}
    for og in range(16):
        for tsl in range(NTS):
            idx = og * NTS + tsl

            def p4_xres(sy, og=og, tsl=tsl, idx=idx):
                if idx >= 2:
                    sy.wait_ge(sDVE.h, fev[idx - 2])
                s_ = dXR[idx % 2]
                s_.inc(sy.dma_start(xstg[idx % 2][:, :],
                                    xT[og * 128:(og + 1) * 128, tsl * 512:(tsl + 1) * 512]), 16)
                xres[idx] = (s_.h, s_.n)
            SP(p4_xres)

            def p4_grp(t, og=og, tsl=tsl, idx=idx):
                if tsl == 0:
                    t.wait_ge(*w4ld[og])
                if idx == 0:
                    t.wait_ge(sDVE.h, mrm[("s", NTS - 1)])
                if idx >= 3:
                    t.wait_ge(sDVE.h, fev[idx - 3])
                rb = bank[idx % 3]
                for cb in range(NCB):
                    t.matmul(rb[:, :], wband4[og % 2][:, cb * 128:(cb + 1) * 128],
                         xn2[:, cb * T + tsl * 512:cb * T + (tsl + 1) * 512],
                         start=(cb == 0), stop=False)
                ins = t.matmul(rb[:, :], loraB_sb[0:48, og * 128:(og + 1) * 128],
                               mrT_sb[0:48, tsl * 512:(tsl + 1) * 512],
                               start=False, stop=True)
                sPE.inc(ins)
                w4grp[(og, tsl)] = sPE.n
            PE(p4_grp)

            def p4_ev(v, og=og, tsl=tsl, idx=idx):
                v.wait_ge(sPE.h, w4grp[(og, tsl)])
                v.wait_ge(*xres[idx])
                if idx >= 2:
                    v.wait_ge(*fst[idx - 2])
                sDVE.inc(v.scalar_tensor_tensor(sstg[idx % 2][:, :], bank[idx % 3][:, :],
                                                biasT_sb[:, og:og + 1], xstg[idx % 2][:, :],
                                                ALU.add, ALU.add))
                fev[idx] = sDVE.n
            DVE(p4_ev)

            def p4_st(sy, og=og, tsl=tsl, idx=idx):
                sy.wait_ge(sDVE.h, fev[idx])
                s_ = dFS[idx % 2]
                s_.inc(sy.dma_start(fspill[og * 128:(og + 1) * 128, tsl * 512:(tsl + 1) * 512],
                                    sstg[idx % 2][:, :]), 16)
                fst[idx] = (s_.h, s_.n)
            SP(p4_st)

        if og + 2 < 16:
            POOL(lambda g, og=og: p4_wload(g, og + 2))

    M["p4_pe"] = sPE.n
    M["p4_act"] = sACT.n
    M["p4_dve"] = sDVE.n
    M["p4_fst"] = [(s_.h, s_.n) for s_ in dFS]

    def _fin(sy):
        for s_ in ALL_SEMS:
            if s_.n:
                sy.wait_ge(s_.h, s_.n)
    if upto <= 4:
        SP(_fin)
        return nc
    g4.close()

    g5 = ExitStack()
    yfull = g5.enter_context(nc.sbuf_tensor("yfull", [128, NCB * T], BF16))
    wband5 = [g5.enter_context(nc.sbuf_tensor(f"wband5_{i}", [128, NCB * 128], BF16))
              for i in range(2)]
    ostg = [g5.enter_context(nc.sbuf_tensor(f"ostg{i}", [128, 512], F32)) for i in range(2)]
    pstg = [g5.enter_context(nc.sbuf_tensor(f"pstg{i}", [128, 512], F32)) for i in range(2)]

    yfld = {}

    def p5_y(sy):
        sy.wait_ge(sCC.h, M["ag"])
        sy.wait_ge(sPE.h, M["p4_pe"])
        sy.wait_ge(sDVE.h, M["p4_dve"])
        for mk in M["p4_fst"]:
            sy.wait_ge(*mk)
        for cc2 in range(NCB):
            rank = cc2 // 16
            col = (cc2 % 16) * 128
            dYF.inc(sy.dma_start(yfull[:, cc2 * T:(cc2 + 1) * T],
                                 agy[rank * T:(rank + 1) * T, col:col + 128], transpose=True), 16)
        M["p5_yf"] = dYF.n
    SP(p5_y)

    w5ld, w5grp = {}, {}

    def p5_wload(g, og):
        if og >= 2:
            g.wait_ge(sPE.h, w5grp[(og - 2, NTS - 1)])
        s_ = dW5[og % 2]
        s_.inc(g.dma_start(
            bass.AP(wband5[og % 2], 0, [[NCB * 128, 128], [128, NCB], [1, 128]]),
            bass.AP(wprojT, og * 128, [[CH, 128], [128 * CH, NCB], [1, 128]])), 16)
        w5ld[og] = (s_.h, s_.n)

    POOL(lambda g: (g.wait_ge(sPE.h, M["p4_pe"]), p5_wload(g, 0), p5_wload(g, 1)))

    oev = {}
    ost = {}
    pld = {}
    for og in range(16):
        for tsl in range(NTS):
            idx = og * NTS + tsl

            def p5_pld(sy, og=og, tsl=tsl, idx=idx):
                if idx >= 2:
                    sy.wait_ge(sDVE.h, oev[idx - 2])
                s_ = dPL[idx % 2]
                s_.inc(sy.dma_start(pstg[idx % 2][:, :],
                                    fspill[og * 128:(og + 1) * 128, tsl * 512:(tsl + 1) * 512]), 16)
                pld[idx] = (s_.h, s_.n)
            SP(p5_pld)

            def p5_grp(t, og=og, tsl=tsl, idx=idx):
                if tsl == 0:
                    t.wait_ge(*w5ld[og])
                if idx == 0:
                    t.wait_ge(dYF.h, M["p5_yf"])
                if idx >= 3:
                    t.wait_ge(sDVE.h, oev[idx - 3])
                rb = bank[idx % 3]
                ins = None
                for cb in range(NCB):
                    sPE.inc(t.matmul(rb[:, :], wband5[og % 2][:, cb * 128:(cb + 1) * 128],
                                   yfull[:, cb * T + tsl * 512:cb * T + (tsl + 1) * 512],
                                   start=(cb == 0), stop=(cb == NCB - 1)))
                w5grp[(og, tsl)] = sPE.n
            PE(p5_grp)

            def p5_ev(v, og=og, tsl=tsl, idx=idx):
                v.wait_ge(sPE.h, w5grp[(og, tsl)])
                v.wait_ge(*pld[idx])
                if idx >= 2:
                    v.wait_ge(*ost[idx - 2])
                sDVE.inc(v.tensor_add(ostg[idx % 2][:, :], bank[idx % 3][:, :],
                                      pstg[idx % 2][:, :]))
                oev[idx] = sDVE.n
            DVE(p5_ev)

            def p5_st(sy, og=og, tsl=tsl, idx=idx):
                sy.wait_ge(sDVE.h, oev[idx])
                s_ = dOUT[idx % 2]
                s_.inc(sy.dma_start(outT[og * 128:(og + 1) * 128, tsl * 512:(tsl + 1) * 512],
                                    ostg[idx % 2][:, :]), 16)
                ost[idx] = (s_.h, s_.n)
            SP(p5_st)

        if og + 2 < 16:
            POOL(lambda g, og=og: p5_wload(g, og + 2))

    SP(lambda sy: (sy.wait_ge(dOUT[0].h, dOUT[0].n), sy.wait_ge(dOUT[1].h, dOUT[1].n)))

    return nc


_CACHE = {}


def _in_maps(inputs):
    x = np.ascontiguousarray(np.asarray(inputs["L0_attn_x"], dtype=np.float32))
    w_attn = np.asarray(inputs["L0_attn_w_attn"], dtype=np.float32)
    w_proj = np.asarray(inputs["L0_attn_w_proj"], dtype=np.float32)
    adapter = np.asarray(inputs["L0_attn_adapter_wte"], dtype=np.float32)
    gating = np.asarray(inputs["L0_attn_gating"], dtype=np.float32)
    ffn_w = np.asarray(inputs["L0_ffn_weight"], dtype=np.float32)
    bias = np.asarray(inputs["L0_ffn_bias"], dtype=np.float32)
    lora_A = np.asarray(inputs["L0_ffn_lora_A"], dtype=np.float32)
    lora_B = np.asarray(inputs["L0_ffn_lora_B"], dtype=np.float32)
    lsi = np.asarray(inputs["L0_ffn_lora_start_indices"]).astype(np.int32)

    cosfm, sinfm, masks_bf, ident, ones1, a48 = _host_consts()
    wqkvT_full = np.ascontiguousarray(w_attn.T)
    wprojT_full = np.ascontiguousarray(w_proj.T)
    wffnT_full = np.ascontiguousarray(ffn_w.T)
    loraAT_full = np.ascontiguousarray(np.transpose(lora_A, (2, 0, 1)).reshape(C, 48))
    loraBT_full = np.ascontiguousarray(np.transpose(lora_B, (0, 2, 1)).reshape(48, C))
    adT_full = np.ascontiguousarray(adapter.T)

    maps = []
    for c in range(8):
        b, g = c // 2, c % 2
        Fg = slice(g * CH, (g + 1) * CH)
        perm = np.r_[g * CH:(g + 1) * CH, (1 - g) * CH:(2 - g) * CH]
        qcols = np.r_[g * CH:(g + 1) * CH, C + g * CH:C + (g + 1) * CH,
                      2 * C + g * CH:2 * C + (g + 1) * CH]
        maps.append({
            "xT": np.ascontiguousarray(x[b].T[perm]),
            "wqkvT": np.ascontiguousarray(wqkvT_full[perm][:, qcols]),
            "wprojT": np.ascontiguousarray(wprojT_full[:, Fg]),
            "wffnT": np.ascontiguousarray(wffnT_full[perm][:, Fg]),
            "biasT": np.ascontiguousarray(bias[Fg].reshape(16, 128).T),
            "loraAT": np.ascontiguousarray(loraAT_full[perm]),
            "loraBT": np.ascontiguousarray(loraBT_full[:, Fg]),
            "adT": np.ascontiguousarray(adT_full[perm]),
            "gat": gating.reshape(1, 1),
            "lsi": lsi.reshape(1, 4),
            "bvec": np.array([[float(b)]], np.float32),
            "cosd": cosfm, "sind": sinfm, "maskd": masks_bf,
            "identd": ident, "ones1d": ones1, "a48d": a48,
            "onesbd": np.ones((128, 1), ml_dtypes.bfloat16),
            "zerod": np.zeros((1, 64), np.float32),
        })
    return maps


def kernel(**inputs):
    maps = _in_maps(inputs)
    if "nc" not in _CACHE:
        _CACHE["nc"] = build_nc()
    res = run_bass_kernel_spmd(_CACHE["nc"], maps, core_ids=list(range(8)))
    out = np.empty((B, T, C), np.float32)
    for c in range(8):
        b, g = c // 2, c % 2
        out[b, :, g * CH:(g + 1) * CH] = res.results[c]["outT"].T
    return out
